# version 15
# baseline (speedup 1.0000x reference)
import numpy as np
from contextlib import ExitStack

import concourse.bass as bass
import concourse.mybir as mybir
from concourse.bass import IndirectOffsetOnAxis as IOA
from concourse.bass_utils import run_bass_kernel_spmd

F32 = mybir.dt.float32
BF16 = mybir.dt.bfloat16
I32 = mybir.dt.int32
AF = mybir.ActivationFunctionType
ALU = mybir.AluOpType
AX = mybir.AxisListType

P = 128
T = 2048
D = 1024
NC = 16
NDC = 8
NH = 16
HD = 64
NE = 8
EPS = 1e-5
BIG = float(1 << 15)

CAPT = [14, 11, 9, 8, 7, 6, 5, 4]
KBASE = [0]
for c in CAPT[:-1]:
    KBASE.append(KBASE[-1] + c)
NTIL = sum(CAPT)
CAPSUM = NTIL * 128
BASE = [k * 128 for k in KBASE]

DMA_SEMS = ("x0", "x1", "w0", "w1", "w2", "init", "initg", "wpm", "sm",
            "out", "xg", "sc", "wg", "xw", "ys", "ga")


def build_nc(debug=False):
    nc = bass.Bass("TRN2", target_bir_lowering=False,
                   detect_race_conditions=False)

    x_d = nc.dram_tensor("x", [T, D], F32, kind="ExternalInput").ap()
    wq_d = nc.dram_tensor("wq", [P, NDC * D], F32, kind="ExternalInput").ap()
    wk_d = nc.dram_tensor("wk", [P, NDC * D], F32, kind="ExternalInput").ap()
    wv_d = nc.dram_tensor("wv", [P, NDC * D], F32, kind="ExternalInput").ap()
    wp_d = nc.dram_tensor("wp", [P, NDC * D], F32, kind="ExternalInput").ap()
    bq_d = nc.dram_tensor("bq", [1, D], F32, kind="ExternalInput").ap()
    gw_d = nc.dram_tensor("gw", [10, D], F32, kind="ExternalInput").ap()
    sm_d = nc.dram_tensor("smp", [65, D], F32, kind="ExternalInput").ap()
    gb_d = nc.dram_tensor("gb", [1, 10], F32, kind="ExternalInput").ap()
    ew_d = nc.dram_tensor("ew", [NE * P, NDC * D], F32, kind="ExternalInput").ap()
    cst_d = nc.dram_tensor("cst", [P, 280], F32, kind="ExternalInput").ap()
    y_d = nc.dram_tensor("y", [T, D], F32, kind="ExternalOutput").ap()
    kind = "ExternalOutput" if debug else "Internal"
    xng_d = nc.dram_tensor("xng", [T, D], BF16, kind=kind).ap()
    xs_d = nc.dram_tensor("xsort", [CAPSUM, D], BF16, kind=kind).ap()
    yv_d = nc.dram_tensor("yvals", [CAPSUM, D], F32, kind=kind).ap()
    xs_r = xs_d.rearrange("(p k) g -> p k g", p=P)
    yv_r = yv_d.rearrange("(p k) g -> p k g", p=P)
    dbg = {}
    if debug:
        dbg["rank"] = nc.dram_tensor("dbg_rank", [P, 128], F32, kind="ExternalOutput").ap()
        dbg["red"] = nc.dram_tensor("dbg_red", [P, 128], F32, kind="ExternalOutput").ap()
        dbg["d0i"] = nc.dram_tensor("dbg_d0i", [P, NC], I32, kind="ExternalOutput").ap()
        dbg["d1i"] = nc.dram_tensor("dbg_d1i", [P, NC], I32, kind="ExternalOutput").ap()
        dbg["widx"] = nc.dram_tensor("dbg_widx", [P, 8], I32, kind="ExternalOutput").ap()
        dbg["srow"] = nc.dram_tensor("dbg_srow", [8, 384], F32, kind="ExternalOutput").ap()
        dbg["msk"] = nc.dram_tensor("dbg_msk", [P, 128], F32, kind="ExternalOutput").ap()
        dbg["s0"] = nc.dram_tensor("dbg_s0", [P, 128], F32, kind="ExternalOutput").ap()
        dbg["r0"] = nc.dram_tensor("dbg_r0", [P, 48], F32, kind="ExternalOutput").ap()
        dbg["ch"] = nc.dram_tensor("dbg_ch", [P, 64], F32, kind="ExternalOutput").ap()
        dbg["vk"] = nc.dram_tensor("dbg_vk", [P, 512], BF16, kind="ExternalOutput").ap()
        dbg["qT"] = nc.dram_tensor("dbg_qT", [P, NDC * T], BF16, kind="ExternalOutput").ap()
        dbg["M"] = nc.dram_tensor("dbg_M", [P, NDC * D], BF16, kind="ExternalOutput").ap()

    with ExitStack() as ctx:
        ent = ctx.enter_context

        xnT_h = ent(nc.sbuf_tensor("xnT", [P, NDC * T], BF16))
        xnT = xnT_h[:]
        xnT_f32 = xnT_h[:].bitcast(F32)
        qT = ent(nc.sbuf_tensor("qT", [P, NDC * T], BF16))[:]
        M_sb = ent(nc.sbuf_tensor("M_sb", [P, NDC * D], BF16))[:]
        W_sb = [ent(nc.sbuf_tensor(f"W{b}", [P, NDC * D], BF16))[:] for b in range(3)]
        acc_h = ent(nc.sbuf_tensor("acc", [P, 8 * D], F32))
        acc = acc_h[:]
        acc_bf = acc_h[:].bitcast(BF16)
        x_pp = [ent(nc.sbuf_tensor(f"xpp{b}", [P, D], F32))[:] for b in range(2)]
        xn_bf = [ent(nc.sbuf_tensor(f"xnbf{b}", [P, D], BF16))[:] for b in range(2)]
        xs_pp = [ent(nc.sbuf_tensor(f"xspp{b}", [P, D], BF16))[:] for b in range(2)]
        xw_pp = [xnT[:, 0:D], xnT[:, D:2 * D]]
        xg_pp = [xnT[:, 2 * D:3 * D], xnT[:, 3 * D:4 * D]]
        yst = [xnT_f32[:, 2048:3072], xnT_f32[:, 3072:4096]]
        gw_bf = ent(nc.sbuf_tensor("gw_bf", [10, D], BF16))[:]
        bq8_bf = ent(nc.sbuf_tensor("bq8_bf", [NDC, P], BF16))[:]
        vk_sb = ent(nc.sbuf_tensor("vk_sb", [P, 512], BF16))[:]
        cb_sb = ent(nc.sbuf_tensor("cb_sb", [9, T], BF16))[:]
        c_all = ent(nc.sbuf_tensor("c_all", [P, NC * NE], F32))[:]
        c_bf = ent(nc.sbuf_tensor("c_bf", [P, NC * NE], BF16))[:]
        msk_all = ent(nc.sbuf_tensor("msk_all", [P, NC * NE], F32))[:]
        s0_all = ent(nc.sbuf_tensor("s0_all", [P, NC * NE], F32))[:]
        ib = ent(nc.sbuf_tensor("ib", [P, 880], F32))[:]
        dbg_sb = ent(nc.sbuf_tensor("dbg_sb", [P, 112], F32))[:] if debug else None
        rw_sb = ent(nc.sbuf_tensor("rw_sb", [P, 32], F32))[:]
        d0i = ent(nc.sbuf_tensor("d0i", [P, NC], I32))[:]
        d1i = ent(nc.sbuf_tensor("d1i", [P, NC], I32))[:]
        widx = ent(nc.sbuf_tensor("widx", [P, 8], I32))[:]
        srow = ent(nc.sbuf_tensor("srow", [8, 384], F32))[:]
        tcol = ent(nc.sbuf_tensor("tcol", [P, 1], F32))[:]
        cst_sb = ent(nc.sbuf_tensor("cst_sb", [P, 280], F32))[:]
        gwT = qT[:, 0:80]
        ones = qT[0:65, 80:208]
        zero = qT[0:1, 208:720]
        bqT = ent(nc.sbuf_tensor("bqT", [P, NDC], F32))[:]
        sm_sb = ent(nc.sbuf_tensor("sm_sb", [65, D], BF16))[:]
        gb_bf = ent(nc.sbuf_tensor("gb_bf", [1, 10], BF16))[:]
        eb_sb = sm_sb[0:9, :]
        bk_bf = sm_sb[32:33, :]
        bv_bf = sm_sb[64:65, :]
        ident = ent(nc.sbuf_tensor("ident", [P, P], BF16))[:]
        cst_eps = ent(nc.sbuf_tensor("cst_eps", [P, 1], F32))[:]
        nc.const_aps.register(F32, EPS, cst_eps)
        egt = [ib[:, 0:10], ib[:, 10:20]]
        pt = [ib[:, 20:30], ib[:, 30:40]]
        m8t = [ib[:, 40:48], ib[:, 48:56]]
        wtst = [ib[:, 56:64], ib[:, 64:72]]
        sm = {}
        for ni, nm in enumerate(("sum", "mean", "ssq", "ssqn", "nvar", "lnv",
                                 "rstd", "m1", "nm1", "Z", "rz", "den", "ds",
                                 "rd")):
            sm[nm] = [ib[:, 72 + 2 * ni + b:73 + 2 * ni + b] for b in range(2)]

        cL = cst_sb[:, 0:128]
        cPM = cst_sb[:, 128:256]
        cIdc = cst_sb[0:8, 256:264]
        cErow = cst_sb[0:8, 264:272]
        cBasem1 = cst_sb[0:8, 272:273]
        cCap = cst_sb[0:8, 273:274]
        cEcol = cst_sb[0:8, 274:275]
        cPcol = cst_sb[:, 275:276]
        cOnecol = cst_sb[:, 276:277]

        sTot = srow[0:1, 0:128]
        sCarry = srow[0:1, 128:256]
        sLoads = srow[0:1, 256:264]
        sKeyrow = srow[0:1, 264:272]
        sSigrow = srow[0:1, 272:280]
        sRankrow = srow[0:1, 280:288]
        sKeyT = srow[0:8, 288:289]
        sRankT = srow[0:8, 289:290]
        sM2 = srow[0:8, 292:300]
        sG2T = srow[0:8, 300:308]
        sSigcol = srow[0:8, 308:309]
        sBrow = srow[0:1, 312:320]
        sCrow = srow[0:1, 320:328]
        sTmp8 = srow[0:8, 328:336]

        ibRank = ib[:, 0:128]
        ibBrep = ib[:, 128:256]
        ibCrep = ib[:, 256:384]
        ibT1 = ib[:, 384:512]
        ibBad = ib[:, 512:640]
        ibPr = ib[:, 640:768]
        rR = [ib[:, 768:784]] * 2
        rAny = [ib[:, 784:800]] * 2
        rBad = [ib[:, 800:816]] * 2
        mR = ib[:, 816:832]
        mAny = ib[:, 832:848]
        mBad = ib[:, 848:864]
        mW = ib[:, 864:880]
        iPm = ib[:, 816:832]
        iKm = ib[:, 832:848]
        iK2 = ib[:, 848:864]
        iRp = ib[:, 832:848]
        iU = ib[:, 864:880]
        iT4 = ib[:, 816:832]
        iRpf = ib[:, 864:880]
        widxf = ib[:, 816:824]
        rW = [rw_sb[:, 0:16], rw_sb[:, 16:32]]

        sq_thr = acc[:, 0:D]
        xm_scr = acc[:, D:2 * D]
        def abf(c0, n):
            return acc_bf[:, c0:c0 + n]
        escr = [abf(4096 + 1024 * b, D) for b in range(2)]
        rscr = [abf(6144 + 1024 * b, D) for b in range(2)]
        mscr = [abf(8192 + 1024 * b, D) for b in range(2)]
        kscr = [abf(10240 + 1024 * b, D) for b in range(2)]
        vscr = [abf(12288 + 1024 * b, D) for b in range(2)]
        escrB = [abf(4096 + 2048 * b, 2048) for b in range(2)]
        rscrB = [abf(8192 + 2048 * b, 2048) for b in range(2)]
        mscrB = [abf(12288 + 2048 * b, 2048) for b in range(2)]
        acc_t = [acc[:, D * il:D * il + D] for il in range(4)]

        P2a = ent(nc.psum_tensor("P2a", [P, D], F32))[:]
        P2b = ent(nc.psum_tensor("P2b", [P, D], F32))[:]
        P2c_h = ent(nc.psum_tensor("P2c", [P, D], F32))
        P2c = P2c_h[:]
        P2c_bf = P2c_h[:].bitcast(BF16)
        P1a = ent(nc.psum_tensor("P1a", [P, 512], F32))[:]
        P1b_h = ent(nc.psum_tensor("P1b", [P, 512], F32))
        P1b = P1b_h[:]
        P1b_bf = P1b_h[:].bitcast(BF16)

        sems = {}
        for nm in ("pe", "dve", "act", "pool") + DMA_SEMS:
            sems[nm] = ent(nc.semaphore(f"s_{nm}"))
        sems["dvex"] = ent(nc.semaphore("s_dvex"))
        sems["actx"] = ent(nc.semaphore("s_actx"))
        _cutc = {"dvex": 0, "actx": 0}

        def mkcut(semname):
            def cut(eng, inst):
                _cutc[semname] += 1
                inst.then_inc(sems[semname], 1)
                eng.wait_ge(sems[semname], _cutc[semname])
            return cut
        vcut = mkcut("dvex")
        acut = mkcut("actx")

        ops = {"pe": [], "dve": [], "act": [], "pool": [], "sync": []}
        cnt = {nm: 0 for nm in ("pe", "dve", "act", "pool") + DMA_SEMS}

        def emit(eng, fn, waits=(), inc=None):
            m = None
            if inc is not None:
                cnt[inc] += 16 if inc in DMA_SEMS else 1
                m = (inc, cnt[inc])
            ops[eng].append((tuple(waits), fn, m))
            return m

        MM = dict(skip_group_check=True)

        regs = {}

        def f_regs(g):
            regs["bcap"] = g.to_reg(CAPSUM - 1)
            regs["bew"] = g.to_reg(NE * P - 1)
            return None
        emit("pool", f_regs)

        def f_init_pool0(g):
            g.memset(ones, 1.0)
            g.memset(zero, 0.0)
            g.memset(cst_eps, EPS)
            g.memset(cb_sb, 1.0)
            return g.memset(ident, 0.0)
        ms_initp0 = emit("pool", f_init_pool0, inc="pool")

        def f_init_pool(g):
            return g.affine_select(
                out=ident, in_=ident, compare_op=ALU.not_equal, fill=1.0,
                base=0, pattern=[[-1, P]], channel_multiplier=1)
        ms_initp = emit("pool", f_init_pool, waits=[ms_initp0], inc="pool")

        ms_cstl = emit("sync", lambda s: s.dma_start(out=cst_sb, in_=cst_d),
                       inc="init")

        ms_gw = emit("pool", lambda g: g.dma_start(out=gw_bf, in_=gw_d),
                     inc="initg")
        emit("pool", lambda g: g.dma_start(
            out=bq8_bf, in_=bq_d.rearrange("o (c p) -> (o c) p", p=P)),
            inc="initg")
        ms_initg2 = ("initg", cnt["initg"])
        ms_wk = emit("pool", lambda g: g.dma_start(out=W_sb[0], in_=wk_d),
                     inc="w0")
        ms_wv = emit("pool", lambda g: g.dma_start(out=W_sb[1], in_=wv_d),
                     inc="w1")
        ms_sm = emit("pool", lambda g: g.dma_start(out=sm_sb, in_=sm_d),
                     inc="sm")
        emit("pool", lambda g: g.dma_start(out=gb_bf, in_=gb_d), inc="init")
        init_all = ("init", cnt["init"])
        ms_wq = emit("pool", lambda g: g.dma_start(out=W_sb[2], in_=wq_d),
                     inc="w2")
        ms_wpm = emit("pool", lambda g: g.dma_start(out=M_sb, in_=wp_d),
                      inc="wpm")

        def f_gwT(t):
            last = None
            for c in range(NDC):
                last = t.matmul(P1b_bf[:, 10 * c:10 * c + 10],
                                gw_bf[0:10, P * c:P * c + P], ident[0:10, 0:10],
                                is_transpose=True, start=True, stop=True, **MM)
            return last
        def f_bqT(t):
            return t.matmul(P1b_bf[:, 128:128 + NDC], bq8_bf,
                            ident[0:NDC, 0:NDC],
                            is_transpose=True, start=True, stop=True, **MM)

        def emit_gwT():
            pe = emit("pe", f_gwT, waits=[ms_gw, ms_initp], inc="pe")
            pe2 = emit("pe", f_bqT, waits=[ms_initg2], inc="pe")
            emit("dve", lambda v: v.tensor_copy(bqT, P1b_bf[:, 128:128 + NDC]),
                 waits=[pe2], inc="dve")
            return emit("dve", lambda v: v.tensor_copy(gwT, P1b_bf[:, 0:80]),
                        waits=[pe], inc="dve")

        ms_xload = {}
        ms_xn = {}
        ms_sq = {}
        ms_trPE = {}
        ms_trEvac = {}
        ms_kpe = {}
        ms_vpe = {}
        ms_gpe = {}
        ms_ek = {}
        ms_rk = {}
        ms_mk = {}
        ms_kt = {}
        ms_vcp = {}
        ms_kvpe = {}
        ms_gd1 = {}
        ms_eg = {}
        ms_gd2 = {}
        ms_xg = {}

        def xnT_sl(d, ii):
            return xnT[:, T * d + P * ii: T * d + P * ii + P]

        def e_front(i):
            b = i % 2
            w = []
            if i >= 2:
                w = [ms_xn[i - 2], ms_sq[i - 2]]
            ms_xload[i] = emit(
                "sync",
                (lambda ii, bb: (lambda s: s.dma_start(
                    out=x_pp[bb], in_=x_d[P * ii:P * ii + P, :])))(i, b),
                waits=w, inc=f"x{b}")

            def f_ln1(v, bb=b):
                vcut(v, v.reduce_sum(sm["sum"][bb], x_pp[bb], AX.X))
                return v.tensor_scalar_mul(sm["mean"][bb], sm["sum"][bb], 1.0 / D)
            emit("dve", f_ln1, waits=[ms_xload[i]])

            ms_sq[i] = emit(
                "act",
                (lambda bb: (lambda s: s.activation(
                    sq_thr, x_pp[bb], AF.Square, accum_out=sm["ssq"][bb])))(b),
                waits=[ms_xload[i]], inc="act")

            def f_ln2(v, bb=b):
                vcut(v, v.tensor_scalar_mul(sm["ssqn"][bb], sm["ssq"][bb], 1.0 / D))
                return v.scalar_tensor_tensor(
                    sm["nvar"][bb], sm["mean"][bb], sm["mean"][bb], sm["ssqn"][bb],
                    ALU.mult, ALU.subtract)
            ms_nvar = emit("dve", f_ln2, waits=[ms_sq[i]], inc="dve")

            def f_rstd(s, bb=b):
                acut(s, s.activation(sm["lnv"][bb], sm["nvar"][bb], AF.Ln,
                                     scale=-1.0, bias=EPS))
                return s.activation(sm["rstd"][bb], sm["lnv"][bb], AF.Exp,
                                    scale=-0.5)
            w = [ms_nvar]
            if i == 0:
                w.append(ms_initp)
            ms_rstd = emit("act", f_rstd, waits=w, inc="act")

            def f_norm(v, bb=b):
                v.tensor_scalar_sub(xm_scr, x_pp[bb], sm["mean"][bb])
                return v.tensor_scalar_mul(xn_bf[bb], xm_scr, sm["rstd"][bb])
            w = [ms_rstd]
            if i >= 2:
                w.append(ms_trPE[i - 2])
                w.append(ms_xg[i - 2])
            ms_xn[i] = emit("dve", f_norm, waits=w, inc="dve")

            ms_xg[i] = emit(
                "act",
                (lambda ii, bb: (lambda s: s.dma_start(
                    out=xng_d[P * ii:P * ii + P, :], in_=xn_bf[bb])))(i, b),
                waits=[ms_xn[i]], inc="xg")

        def e_tr(i):
            b = i % 2
            tr_bank = P2c_bf[:, 1024 * b:1024 * b + 1024]

            def f_tr(t, bb=b, trb=tr_bank):
                last = None
                for c in range(NDC):
                    last = t.matmul(trb[:, P * c:P * c + P],
                                    xn_bf[bb][:, P * c:P * c + P], ident,
                                    is_transpose=True, start=True, stop=True, **MM)
                return last
            w = [ms_xn[i]]
            if i >= 2:
                w.append(ms_trEvac[i - 2])
            ms_trPE[i] = emit("pe", f_tr, waits=w, inc="pe")

            xnT_dst = xnT.rearrange("p (d t) -> p d t", t=T)[:, :, P * i:P * i + P]

            def f_trev(v, trb=tr_bank, dst=xnT_dst):
                src_ = trb.rearrange("p (d t) -> p d t", t=P)
                return v.tensor_copy(dst, src_)
            ms_trEvac[i] = emit("dve", f_trev, waits=[ms_trPE[i]], inc="dve")

        def e_kv(i):
            b = i % 2

            def f_kv(t, ii=i, bb=b):
                last = None
                if ii == 0:
                    t.matmul(P1a, ones[0:1, 0:P], zero[0:1, 0:512],
                             start=True, stop=False, **MM)
                for h in range(NH):
                    pr = h // 2
                    par = h % 2
                    out = P1a[64 * par:64 * par + 64, 64 * pr:64 * pr + 64]
                    last = t.matmul(out, vscr[bb][:, HD * h:HD * h + HD],
                                    kscr[bb][:, HD * h:HD * h + HD],
                                    start=False,
                                    stop=(ii == NC - 1 and h == NH - 1), **MM)
                return last
            ms_kvpe[i] = emit("pe", f_kv, waits=[ms_kt[i], ms_vcp[i]], inc="pe")

        def e_mms(i):
            b = i % 2

            def f_g(t, ii=i):
                t.matmul(P1b[:, 0:10], ones[0:1, 0:P], gb_bf[0:1, 0:10],
                         start=True, stop=False, **MM)
                last = None
                for d in range(NDC):
                    last = t.matmul(P1b[:, 0:10], xnT_sl(d, ii),
                                    gwT[:, 10 * d:10 * d + 10],
                                    start=False, stop=d == NDC - 1, **MM)
                return last

            def f_k(t, ii=i):
                last = None
                for d in range(NDC):
                    t.matmul(P2a[:, 0:512], xnT_sl(d, ii),
                             W_sb[0][:, D * d:D * d + 512],
                             start=d == 0, stop=False, **MM)
                    last = t.matmul(P2a[:, 512:1024], xnT_sl(d, ii),
                                    W_sb[0][:, D * d + 512:D * d + 1024],
                                    start=d == 0, stop=False, **MM)
                return last

            def f_kb(t):
                t.matmul(P2a[:, 0:512], ones[32:33, 0:P], bk_bf[0:1, 0:512],
                         start=False, stop=True, **MM)
                return t.matmul(P2a[:, 512:1024], ones[32:33, 0:P],
                                bk_bf[0:1, 512:1024],
                                start=False, stop=True, **MM)

            def f_v(t, ii=i):
                last = None
                for d in range(NDC):
                    t.matmul(P2b[:, 0:512], xnT_sl(d, ii),
                             W_sb[1][:, D * d:D * d + 512],
                             start=d == 0, stop=False, **MM)
                    last = t.matmul(P2b[:, 512:1024], xnT_sl(d, ii),
                                    W_sb[1][:, D * d + 512:D * d + 1024],
                                    start=d == 0, stop=False, **MM)
                return last

            def f_vb(t):
                t.matmul(P2b[:, 0:512], ones[64:65, 0:P], bv_bf[0:1, 0:512],
                         start=False, stop=True, **MM)
                return t.matmul(P2b[:, 512:1024], ones[64:65, 0:P],
                                bv_bf[0:1, 512:1024],
                                start=False, stop=True, **MM)

            w = [ms_trEvac[i]]
            if i == 0:
                w += [ms_wk]
            else:
                w += [ms_rk[i - 1]]
            emit("pe", f_k, waits=w, inc="pe")
            ms_kpe[i] = emit("pe", f_kb,
                             waits=[ms_sm] if i == 0 else (), inc="pe")
            w = [ms_trEvac[i]]
            if i == 0:
                w.append(ms_wv)
            else:
                w.append(ms_vcp[i - 1])
            emit("pe", f_v, waits=w, inc="pe")
            ms_vpe[i] = emit("pe", f_vb,
                             waits=[ms_sm] if i == 0 else (), inc="pe")
            w = [ms_trEvac[i]]
            if i == 0:
                w += [ms_gwT, init_all]
            else:
                w += [ms_eg[i - 1], ms_gd1[i - 1]]
            ms_gpe[i] = emit("pe", f_g, waits=w, inc="pe")

        def e_evacs(i):
            b = i % 2
            ms_ek[i] = emit(
                "act",
                (lambda bb: (lambda s: s.activation(escr[bb], P2a, AF.Exp)))(b),
                waits=[ms_kpe[i]] + ([ms_mk[i - 2]] if i >= 2 else []),
                inc="act")
            ms_rk[i] = emit(
                "act",
                (lambda bb: (lambda s: s.activation(rscr[bb], P2a, AF.Relu)))(b),
                waits=[ms_kpe[i]], inc="act")
            ms_mk[i] = emit(
                "dve",
                (lambda bb: (lambda v: v.tensor_scalar_min(mscr[bb], escr[bb], 1.0)))(b),
                waits=[ms_ek[i]] + ([ms_kt[i - 2]] if i >= 2 else []),
                inc="dve")
            w = [ms_mk[i], ms_rk[i]]
            if i >= 2:
                w.append(ms_kvpe[i - 2])
            ms_kt[i] = emit(
                "dve",
                (lambda bb: (lambda v: v.tensor_tensor(
                    kscr[bb], rscr[bb], mscr[bb], ALU.add)))(b),
                waits=w, inc="dve")
            w = [ms_vpe[i]]
            if i >= 2:
                w.append(ms_kvpe[i - 2])
            ms_vcp[i] = emit(
                "act",
                (lambda bb: (lambda s: s.copy(vscr[bb], P2b)))(b),
                waits=w, inc="act")

        def e_gate(i):
            b = i % 2

            def f_gd1(v, bb=b):
                vcut(v, v.reduce_max(sm["m1"][bb], P1b[:, 0:10], AX.X))
                return v.tensor_scalar_mul(sm["nm1"][bb], sm["m1"][bb], -1.0)
            ms_gd1[i] = emit("dve", f_gd1, waits=[ms_gpe[i]], inc="dve")

            w = [ms_gd1[i]]
            if i >= 2:
                w.append(ms_gd2[i - 2])
            ms_eg[i] = emit(
                "act",
                (lambda bb: (lambda s: s.activation(
                    egt[bb], P1b[:, 0:10], AF.Exp, bias=sm["nm1"][bb])))(b),
                waits=w, inc="act")

            def f_gd2(v, bb=b, ii=i):
                vcut(v, v.reduce_sum(sm["Z"][bb], egt[bb], AX.X))
                vcut(v, v.reciprocal(sm["rz"][bb], sm["Z"][bb]))
                vcut(v, v.tensor_scalar_mul(pt[bb], egt[bb], sm["rz"][bb]))
                vcut(v, v.max(m8t[bb], pt[bb]))
                vcut(v, v.tensor_single_scalar(msk_all[:, 8 * ii:8 * ii + 8],
                                               pt[bb][:, 0:8],
                                               m8t[bb][:, 1:2], ALU.is_ge))
                vcut(v, v.tensor_single_scalar(s0_all[:, 8 * ii:8 * ii + 8],
                                               pt[bb][:, 0:8],
                                               m8t[bb][:, 0:1], ALU.is_ge))
                vcut(v, v.tensor_tensor(wtst[bb], msk_all[:, 8 * ii:8 * ii + 8],
                                        pt[bb][:, 0:8], ALU.mult))
                vcut(v, v.reduce_sum(sm["den"][bb], wtst[bb], AX.X))
                vcut(v, v.tensor_scalar_max(sm["ds"][bb], sm["den"][bb], 1e-30))
                vcut(v, v.reciprocal(sm["rd"][bb], sm["ds"][bb]))
                vcut(v, v.tensor_scalar_mul(c_all[:, 8 * ii:8 * ii + 8], wtst[bb],
                                            sm["rd"][bb]))
                return v.tensor_copy(c_bf[:, 8 * ii:8 * ii + 8],
                                     c_all[:, 8 * ii:8 * ii + 8])
            ms_gd2[i] = emit("dve", f_gd2, waits=[ms_eg[i]], inc="dve")

        e_front(0)
        e_front(1)
        e_tr(0)
        ms_gwT = emit_gwT()
        for i in range(NC):
            if i + 2 < NC:
                e_front(i + 2)
            if i >= 1:
                e_kv(i - 1)
            if i + 1 < NC:
                e_tr(i + 1)
            e_mms(i)
            e_evacs(i)
            e_gate(i)
        e_kv(NC - 1)

        def f_ct(t):
            last = None
            for i in range(NC):
                last = t.matmul(P2c_bf[0:8, P * i:P * i + P],
                                c_bf[:, 8 * i:8 * i + 8], ident,
                                is_transpose=True, start=True, stop=True, **MM)
            return last
        ms_ctpe = emit("pe", f_ct,
                       waits=[ms_gd2[NC - 1], ms_trEvac[NC - 1],
                              ms_trEvac[NC - 2]],
                       inc="pe")
        ms_cb = emit("dve", lambda v: v.tensor_copy(cb_sb[0:8, :], P2c_bf[0:8, :]),
                     waits=[ms_ctpe, ms_initp0], inc="dve")
        ms_vk = emit("dve", lambda v: v.tensor_copy(vk_sb, P1a),
                     waits=[ms_kvpe[NC - 1]], inc="dve")

        ms_Mpe = {}
        ms_Mev = {}
        for pr in range(NDC):
            ps = P2a if pr % 2 == 0 else P2b

            def f_M(t, pp=pr, ps_=ps):
                last = None
                for par in range(2):
                    r0 = 64 * par
                    for hh in range(2):
                        last = t.matmul(
                            ps_[r0:r0 + 64, 512 * hh:512 * hh + 512],
                            vk_sb[r0:r0 + 64, 64 * pp:64 * pp + 64],
                            M_sb[r0:r0 + 64, D * pp + 512 * hh:D * pp + 512 * hh + 512],
                            start=True, stop=True, **MM)
                return last
            if pr == 0:
                w = [ms_vk, ms_wpm, ms_rk[NC - 1]]
            elif pr == 1:
                w = [ms_vcp[NC - 1]]
            else:
                w = [ms_Mev[pr - 2]]
            ms_Mpe[pr] = emit("pe", f_M, waits=w, inc="pe")
            ms_Mev[pr] = emit(
                "dve",
                (lambda pp, ps_: (lambda v: v.tensor_copy(
                    M_sb[:, D * pp:D * pp + D], ps_)))(pr, ps),
                waits=[ms_Mpe[pr]], inc="dve")

        ms_i1 = emit("pe", lambda t: t.matmul(
            P2c[0:1, 0:128], cOnecol, msk_all, start=True, stop=True, **MM),
            waits=[ms_cb, ms_gd2[NC - 1], ms_cstl], inc="pe")
        ms_i2 = emit("dve", lambda v: v.tensor_copy(sTot, P2c[0:1, 0:128]),
                     waits=[ms_i1], inc="dve")
        ms_i3 = emit("pe", lambda t: t.matmul(
            P2c[:, 128:129], sTot, cIdc[0:1, 0:1],
            is_transpose=True, start=True, stop=True, **MM),
            waits=[ms_i2], inc="pe")
        ms_i4 = emit("dve", lambda v: v.tensor_copy(tcol, P2c[:, 128:129]),
                     waits=[ms_i3], inc="dve")
        ms_i5 = emit("pe", lambda t: t.matmul(
            P2c[0:1, 256:384], tcol, cPM, start=True, stop=True, **MM),
            waits=[ms_i4], inc="pe")

        def f_i6(v):
            vcut(v, v.tensor_copy(sCarry, P2c[0:1, 256:384]))
            vcut(v, v.tensor_tensor(sLoads, sCarry[0:1, 120:128],
                                    sTot[0:1, 120:128], ALU.add))
            vcut(v, v.tensor_scalar_mul(sKeyrow, sLoads, 8.0))
            return v.tensor_tensor(sKeyrow, sKeyrow, cErow[0:1, 0:8], ALU.add)
        ms_i6 = emit("dve", f_i6, waits=[ms_i5], inc="dve")

        def f_rank(t):
            last = None
            for c in range(NC):
                t.matmul(P1a[:, 8 * c:8 * c + 8], cL[0:1, 0:128],
                         sCarry[0:1, 8 * c:8 * c + 8],
                         start=True, stop=False, **MM)
                last = t.matmul(P1a[:, 8 * c:8 * c + 8], cL,
                                msk_all[:, 8 * c:8 * c + 8],
                                start=False, stop=True, **MM)
            return last
        ms_i7 = emit("pe", f_rank, waits=[ms_i6, ms_vk], inc="pe")

        ms_i8 = emit("pe", lambda t: t.matmul(
            P2c[0:8, 132:133], sKeyrow, cIdc[0:1, 0:1],
            is_transpose=True, start=True, stop=True, **MM),
            waits=[ms_i6], inc="pe")
        ms_i9 = emit("pe", lambda t: t.matmul(
            P2c[0:8, 136:144], cL[0:1, 0:8], sKeyrow,
            start=True, stop=True, **MM),
            waits=[ms_i6], inc="pe")

        def f_i10(v):
            vcut(v, v.tensor_copy(sKeyT, P2c[0:8, 132:133]))
            vcut(v, v.tensor_single_scalar(sM2, P2c[0:8, 136:144], sKeyT,
                                           ALU.is_gt))
            return v.reduce_sum(sRankT, sM2, AX.X)
        ms_i10 = emit("dve", f_i10, waits=[ms_i8, ms_i9], inc="dve")

        ms_i11 = emit("pe", lambda t: t.matmul(
            P2c[0:1, 160:168], sRankT, cIdc, is_transpose=True,
            start=True, stop=True, **MM), waits=[ms_i10], inc="pe")
        ms_i12 = emit("dve", lambda v: v.tensor_copy(sRankrow, P2c[0:1, 160:168]),
                      waits=[ms_i11], inc="dve")
        ms_i13 = emit("pe", lambda t: t.matmul(
            P2c[0:8, 168:176], cL[0:1, 0:8], sRankrow,
            start=True, stop=True, **MM), waits=[ms_i12], inc="pe")

        def f_i14(v):
            vcut(v, v.tensor_single_scalar(sG2T, P2c[0:8, 168:176], cEcol,
                                           ALU.is_equal))
            vcut(v, v.tensor_tensor(sTmp8, sG2T, cErow, ALU.mult))
            return v.reduce_sum(sSigcol, sTmp8, AX.X)
        ms_i14 = emit("dve", f_i14, waits=[ms_i13], inc="dve")

        ms_i15 = emit("pe", lambda t: t.matmul(
            P2c[0:1, 176:184], sSigcol, cIdc, is_transpose=True,
            start=True, stop=True, **MM), waits=[ms_i14], inc="pe")
        ms_i16 = emit("dve", lambda v: v.tensor_copy(sSigrow, P2c[0:1, 176:184]),
                      waits=[ms_i15], inc="dve")
        ms_i17 = emit("pe", lambda t: t.matmul(
            P1b[:, 0:8], cL[0:1, 0:128], sSigrow, start=True, stop=True, **MM),
            waits=[ms_i16, ms_gd2[NC - 1], ms_eg[NC - 1]], inc="pe")

        def f_i18(v):
            vcut(v, v.tensor_scalar_mul(widxf, P1b[:, 0:8], 128.0))
            vcut(v, v.tensor_single_scalar(ib[:, 824:832], widxf, cPcol,
                                           ALU.add))
            return v.tensor_copy(widx, ib[:, 824:832])
        ms_widx = emit("dve", f_i18, waits=[ms_i17], inc="dve")

        ms_i19 = emit("pe", lambda t: t.matmul(
            P2c[0:1, 184:192], cBasem1, sG2T, start=True, stop=True, **MM),
            waits=[ms_i14], inc="pe")
        ms_i20 = emit("pe", lambda t: t.matmul(
            P2c[0:1, 192:200], cCap, sG2T, start=True, stop=True, **MM),
            waits=[ms_i14], inc="pe")

        def f_i21(v):
            vcut(v, v.tensor_copy(sBrow, P2c[0:1, 184:192]))
            return v.tensor_copy(sCrow, P2c[0:1, 192:200])
        ms_i21 = emit("dve", f_i21, waits=[ms_i19, ms_i20], inc="dve")

        def f_i22(t):
            last = None
            for c in range(NC):
                t.matmul(P1b[:, 128 + 8 * c:128 + 8 * c + 8],
                         cL[0:1, 0:128], sBrow, start=True, stop=True, **MM)
                last = t.matmul(P1b[:, 256 + 8 * c:256 + 8 * c + 8],
                                cL[0:1, 0:128], sCrow,
                                start=True, stop=True, **MM)
            return last
        ms_i22 = emit("pe", f_i22, waits=[ms_i21], inc="pe")
        ms_i23 = ms_i22

        def f_i24(v):
            v.tensor_copy(ibRank, P1a[:, 0:128])
            v.tensor_copy(ibBrep, P1b[:, 128:256])
            return v.tensor_copy(ibCrep, P1b[:, 256:384])
        ms_idxevac = emit("dve", f_i24, waits=[ms_i7, ms_i22, ms_i23],
                          inc="dve")

        def seg(ap):
            return ap.rearrange("p (k e) -> p k e", e=8)

        def f_redset(v, msks, oR, oAny, oBad, oW):
            vcut(v, v.tensor_tensor(ibPr, msks, ibT1, ALU.mult))
            vcut(v, v.reduce_sum(oR, seg(ibPr), AX.X))
            vcut(v, v.reduce_sum(oAny, seg(msks), AX.X))
            vcut(v, v.tensor_tensor(ibPr, msks, ibBad, ALU.mult))
            vcut(v, v.reduce_sum(oBad, seg(ibPr), AX.X))
            vcut(v, v.tensor_tensor(ibPr, msks, c_all, ALU.mult))
            vcut(v, v.reduce_sum(oW, seg(ibPr), AX.X))

        def f_rpchain(v, s):
            vcut(v, v.tensor_scalar_mul(iPm, rR[s], 1.0 / 128.0))
            vcut(v, v.tensor_scalar_add(iKm, iPm, -0.49609375))
            vcut(v, v.tensor_scalar_add(iK2, iKm, 12582912.0))
            vcut(v, v.tensor_scalar_add(iKm, iK2, -12582912.0))
            vcut(v, v.scalar_tensor_tensor(iPm, iKm, -128.0, rR[s],
                                           ALU.mult, ALU.add))
            vcut(v, v.scalar_tensor_tensor(iK2, iPm, 64.0, iKm,
                                           ALU.mult, ALU.add))
            vcut(v, v.tensor_tensor(iU, rBad[s], rAny[s], ALU.subtract))
            vcut(v, v.scalar_tensor_tensor(iT4, iU, BIG, iK2,
                                           ALU.mult, ALU.add))
            vcut(v, v.tensor_scalar_add(iRpf, iT4, BIG))
            vcut(v, v.tensor_scalar_mul(iKm, iRpf, -1.0))
            vcut(v, v.tensor_tensor(iT4, iKm, iRpf, ALU.max))
            last = v.tensor_copy(d0i if s == 0 else d1i, iT4)
            if s == 0:
                vcut(v, last)
            return last

        def f_dch(v):
            vcut(v, v.tensor_tensor(ibT1, ibRank, ibBrep, ALU.add))
            vcut(v, v.tensor_tensor(ibBad, ibRank, ibCrep, ALU.is_gt))
            f_redset(v, s0_all, rR[0], rAny[0], rBad[0], rW[0])
            if debug:
                vcut(v, v.tensor_copy(dbg_sb[:, 0:48], ib[:, 768:816]))
            f_rpchain(v, 0)
            if debug:
                vcut(v, v.tensor_copy(dbg_sb[:, 48:112], ib[:, 816:880]))
            f_redset(v, msk_all, mR, mAny, mBad, mW)
            vcut(v, v.tensor_tensor(rW[1], mW, rW[0], ALU.subtract))
            vcut(v, v.tensor_tensor(rR[1], mR, rR[0], ALU.subtract))
            vcut(v, v.tensor_tensor(rAny[1], mAny, rAny[0], ALU.subtract))
            vcut(v, v.tensor_tensor(rBad[1], mBad, rBad[0], ALU.subtract))
            return f_rpchain(v, 1)
        ms_dch = emit("dve", f_dch, waits=[ms_idxevac], inc="dve")

        ms_wg = {}
        wg_usage = {0: ms_kpe[NC - 1], 1: ms_vpe[NC - 1]}

        def emit_wg(j):
            w = [ms_widx, wg_usage[j]]
            ms_wg[j] = emit(
                "pool",
                (lambda jj: (lambda g: g.indirect_dma_start(
                    out=W_sb[jj % 3], out_offset=None, in_=ew_d,
                    in_offset=IOA(widx[:, jj:jj + 1], 0),
                    bounds_check=regs["bew"], oob_is_err=False)))(j),
                waits=w, inc="wg")

        emit_wg(0)
        emit_wg(1)

        ms_rl = {}
        ms_xsc = {}
        ms_sct = {}
        for i in range(NC):
            b = i % 2
            if i < 2:
                w = [ms_xg[NC - 2 + i], ms_trPE[NC - 2 + i]]
            else:
                w = [ms_xg[i], ms_xsc[i - 2]]
            ms_rl[i] = emit(
                "sync",
                (lambda ii, bb: (lambda s: s.dma_start(
                    out=xn_bf[bb], in_=xng_d[P * ii:P * ii + P, :])))(i, b),
                waits=w, inc=f"x{b}")

            def f_xs(g, ii=i, bb=b):
                g.tensor_scalar_mul(xs_pp[0], xn_bf[bb], rW[0][:, ii:ii + 1])
                return g.tensor_scalar_mul(xs_pp[1], xn_bf[bb],
                                           rW[1][:, ii:ii + 1])
            w = [ms_rl[i], ms_dch]
            if i >= 1:
                w.append(ms_sct[i - 1])
            ms_xsc[i] = emit("pool", f_xs, waits=w, inc="pool")

            for s in range(2):
                di = d0i if s == 0 else d1i
                ms_sct[i] = emit(
                    "pool",
                    (lambda ii, ss, di_: (lambda g: g.indirect_dma_start(
                        out=xs_d, out_offset=IOA(di_[:, ii:ii + 1], 0),
                        in_=xs_pp[ss], in_offset=None,
                        bounds_check=regs["bcap"], oob_is_err=False)))(i, s, di),
                    waits=[ms_xsc[i]], inc="sc")
        ms_scall = ("sc", cnt["sc"])

        ms_qpe = {}
        ms_eqa = {}
        ms_rqd = {}
        ms_mq = {}
        ms_qadd = {}
        for f in range(NDC):
            fb = f % 2
            if fb == 0:
                wins = [P2a[:, 0:512], P2a[:, 512:1024],
                        P2b[:, 0:512], P2b[:, 512:1024]]
            else:
                wins = [P2c[:, 0:512], P2c[:, 512:1024], P1a, P1b]

            def f_q(t, ff=f, ws=wins):
                last = None
                for d in range(NDC):
                    for wdw in range(4):
                        last = t.matmul(
                            ws[wdw],
                            W_sb[2][:, D * d + P * ff:D * d + P * ff + P],
                            xnT[:, T * d + 512 * wdw:T * d + 512 * wdw + 512],
                            start=d == 0, stop=d == NDC - 1, **MM)
                return last
            if f == 0:
                w = [ms_wq, ms_Mev[NDC - 1]]
            elif f == 1:
                w = [ms_idxevac, ms_widx]
            else:
                w = [ms_rqd[f - 2]]
            ms_qpe[f] = emit("pe", f_q, waits=w, inc="pe")

            def f_eq(s, ws=wins, fb_=fb, ff=f):
                last = None
                for wdw in range(4):
                    last = s.activation(escrB[fb_][:, 512 * wdw:512 * wdw + 512],
                                        ws[wdw], AF.Exp, bias=bqT[:, ff:ff + 1])
                return last
            w = [ms_qpe[f]]
            if f >= 2:
                w.append(ms_mq[f - 2])
            if f in (0, 1):
                w += [ms_kvpe[NC - 1], ms_kt[NC - 1], ms_mk[NC - 1]]
            ms_eqa[f] = emit("act", f_eq, waits=w, inc="act")

            def f_rq(s, ws=wins, fb_=fb, ff=f):
                last = None
                for wdw in range(4):
                    last = s.activation(
                        rscrB[fb_][:, 512 * wdw:512 * wdw + 512], ws[wdw],
                        AF.Relu, bias=bqT[:, ff:ff + 1])
                return last
            w = [ms_qpe[f]]
            if f >= 2:
                w.append(ms_qadd[f - 2])
            if f in (0, 1):
                w += [ms_kvpe[NC - 1], ms_kt[NC - 1], ms_mk[NC - 1]]
            ms_rqd[f] = emit("act", f_rq, waits=w, inc="act")

            w = [ms_eqa[f]]
            if f >= 2:
                w.append(ms_qadd[f - 2])
            if f in (0, 1):
                w += [ms_kvpe[NC - 1], ms_kt[NC - 1]]
            ms_mq[f] = emit(
                "dve",
                (lambda fb_: (lambda v: v.tensor_scalar_min(
                    mscrB[fb_], escrB[fb_], 1.0)))(fb),
                waits=w, inc="dve")

            ms_qadd[f] = emit(
                "dve",
                (lambda fb_, ff: (lambda v: v.tensor_tensor(
                    qT[:, T * ff:T * ff + T], rscrB[fb_], mscrB[fb_], ALU.add)))(fb, f),
                waits=[ms_mq[f], ms_rqd[f]], inc="dve")

        ms_Etr = {}
        ms_Etev = {}
        ms_Emm = {}
        ms_Eyev = {}
        ms_Eyst = {}
        ms_Exw = {}
        wg_usage[2] = ms_qpe[NDC - 1]

        tt = 0
        for j in range(NE):
            if j == 2:
                emit_wg(2)
            for kk in range(CAPT[j]):
                b = tt % 2
                kg = KBASE[j] + kk
                w = []
                if tt == 0:
                    w.append(ms_scall)
                if tt in (0, 1):
                    w.append(ms_qpe[NDC - 1])
                if tt >= 2:
                    w.append(ms_Etr[tt - 2])
                ms_Exw[tt] = emit(
                    "act",
                    (lambda kg_, bb: (lambda s: s.dma_start(
                        out=xw_pp[bb], in_=xs_r[:, kg_:kg_ + 1, :])))(kg, b),
                    waits=w, inc="xw")

                tr_bank = P2c_bf[:, 1024 * b:1024 * b + 1024]

                def f_tr(t, bb=b, trb=tr_bank):
                    last = None
                    for c in range(NDC):
                        last = t.matmul(trb[:, P * c:P * c + P],
                                        xw_pp[bb][:, P * c:P * c + P], ident,
                                        is_transpose=True, start=True,
                                        stop=True, **MM)
                    return last
                w = [ms_Exw[tt]]
                if tt == 0:
                    w += [ms_eqa[NDC - 1], ms_rqd[NDC - 1]]
                if tt >= 2:
                    w.append(ms_Etev[tt - 2])
                ms_Etr[tt] = emit("pe", f_tr, waits=w, inc="pe")

                w = [ms_Etr[tt]]
                if tt >= 2:
                    w.append(ms_Emm[tt - 2])
                ms_Etev[tt] = emit(
                    "dve",
                    (lambda bb, trb=tr_bank: (lambda v: v.tensor_copy(
                        xg_pp[bb], trb)))(b),
                    waits=w, inc="dve")

                ps = P2a if b == 0 else P2b

                def f_emm(t, bb=b, ps_=ps, jj=j):
                    last = None
                    for d in range(NDC):
                        t.matmul(ps_[:, 0:512],
                                 xg_pp[bb][:, P * d:P * d + P],
                                 W_sb[jj % 3][:, D * d:D * d + 512],
                                 start=d == 0, stop=d == NDC - 1, **MM)
                        last = t.matmul(ps_[:, 512:1024],
                                        xg_pp[bb][:, P * d:P * d + P],
                                        W_sb[jj % 3][:, D * d + 512:D * d + 1024],
                                        start=d == 0, stop=d == NDC - 1, **MM)
                    return last
                w = [ms_Etev[tt]]
                if kk == 0:
                    w.append(ms_wg[j])
                if tt in (0, 1):
                    w += [ms_eqa[NDC - 2], ms_rqd[NDC - 2]]
                if tt >= 2:
                    w.append(ms_Eyev[tt - 2])
                ms_Emm[tt] = emit("pe", f_emm, waits=w, inc="pe")
                if j + 3 < NE and kk == CAPT[j] - 1:
                    wg_usage[j + 3] = ms_Emm[tt]
                    emit_wg(j + 3)

                w = [ms_Emm[tt]]
                if tt >= 2:
                    w.append(ms_Eyst[tt - 2])
                ms_Eyev[tt] = emit(
                    "act",
                    (lambda bb, ps_=ps: (lambda s: s.copy(yst[bb], ps_)))(b),
                    waits=w, inc="act")

                ms_Eyst[tt] = emit(
                    "sync",
                    (lambda kg_, bb: (lambda s: s.dma_start(
                        out=yv_r[:, kg_:kg_ + 1, :], in_=yst[bb])))(kg, b),
                    waits=[ms_Eyev[tt]], inc="ys")
                tt += 1
        ms_ysall = ("ys", cnt["ys"])

        ms_dxl = {}
        ms_dev = {}
        ms_dga = {}
        ms_dst = {}
        for gi in range(NC):
            b = gi % 2
            il = gi % 4
            w = []
            if gi < 2:
                w = [ms_xn[NC - 2 + gi], ms_sq[NC - 2 + gi], ms_rl[NC - 2 + gi]]
            else:
                w = [ms_dev[gi - 2]]
            ms_dxl[gi] = emit(
                "sync",
                (lambda g_, bb: (lambda s: s.dma_start(
                    out=x_pp[bb], in_=x_d[P * g_:P * g_ + P, :])))(gi, b),
                waits=w, inc=f"x{b}")

            ps = P2a if b == 0 else P2b

            def f_j0(t, g_=gi, ps_=ps):
                t.matmul(ps_[:, 0:512], cb_sb[0:9, P * g_:P * g_ + P],
                         eb_sb[0:9, 0:512], start=True, stop=False, **MM)
                t.matmul(ps_[:, 512:1024], cb_sb[0:9, P * g_:P * g_ + P],
                         eb_sb[0:9, 512:1024], start=True, stop=False, **MM)
                last = None
                for d in range(NDC):
                    st = d == NDC - 1
                    t.matmul(ps_[:, 0:512],
                             qT[:, T * d + P * g_:T * d + P * g_ + P],
                             M_sb[:, D * d:D * d + 512],
                             start=False, stop=st, **MM)
                    last = t.matmul(ps_[:, 512:1024],
                                    qT[:, T * d + P * g_:T * d + P * g_ + P],
                                    M_sb[:, D * d + 512:D * d + 1024],
                                    start=False, stop=st, **MM)
                return last
            w = [ms_Eyev[NTIL - 2 + b]]
            if gi == 0:
                w += [ms_qadd[NDC - 1], ms_Mev[NDC - 1], ms_cb]
            if gi >= 2:
                w.append(ms_dev[gi - 2])
            mpe = emit("pe", f_j0, waits=w, inc="pe")

            def f_dev(v, g_=gi, il_=il, ps_=ps, b_=b):
                return v.tensor_tensor(acc_t[il_], ps_, x_pp[b_], ALU.add)
            w = [mpe, ms_dxl[gi]]
            if gi in (0, 1):
                w += [ms_xn[NC - 1], ms_sq[NC - 1]]
            if gi == 2:
                w.append(ms_mq[NDC - 2])
            if gi == 3:
                w.append(ms_mq[NDC - 1])
            if gi >= 4:
                w.append(ms_dst[gi - 4])
            ms_dev[gi] = emit("dve", f_dev, waits=w, inc="dve")

            w = [ms_dev[gi]]
            if gi == 0:
                w.append(ms_ysall)
            emit("pool",
                 (lambda g_, il_: (lambda g: g.indirect_dma_start(
                     out=acc_t[il_], out_offset=None, in_=yv_d,
                     in_offset=IOA(d0i[:, g_:g_ + 1], 0),
                     bounds_check=regs["bcap"], oob_is_err=False,
                     compute_op=ALU.add)))(gi, il),
                 waits=w, inc="ga")
            ga0 = ("ga", cnt["ga"])
            ms_dga[gi] = emit(
                "pool",
                (lambda g_, il_: (lambda g: g.indirect_dma_start(
                    out=acc_t[il_], out_offset=None, in_=yv_d,
                    in_offset=IOA(d1i[:, g_:g_ + 1], 0),
                    bounds_check=regs["bcap"], oob_is_err=False,
                    compute_op=ALU.add)))(gi, il),
                waits=[ga0], inc="ga")

            ms_dst[gi] = emit(
                "sync",
                (lambda g_, il_: (lambda s: s.dma_start(
                    out=y_d[P * g_:P * g_ + P, :], in_=acc_t[il_])))(gi, il),
                waits=[ms_dga[gi]], inc="out")

        if debug:
            emit("sync", lambda s: s.dma_start(out=dbg["rank"], in_=ibRank),
                 waits=[ms_dch], inc="out")
            emit("sync", lambda s: s.dma_start(out=dbg["red"], in_=ib[:, 128:256]),
                 waits=[ms_dch], inc="out")
            emit("sync", lambda s: s.dma_start(out=dbg["d0i"], in_=d0i),
                 waits=[ms_dch], inc="out")
            emit("sync", lambda s: s.dma_start(out=dbg["d1i"], in_=d1i),
                 waits=[ms_dch], inc="out")
            emit("sync", lambda s: s.dma_start(out=dbg["widx"], in_=widx),
                 waits=[ms_widx], inc="out")
            emit("sync", lambda s: s.dma_start(out=dbg["srow"], in_=srow),
                 waits=[ms_dch], inc="out")
            emit("sync", lambda s: s.dma_start(out=dbg["msk"], in_=msk_all),
                 waits=[ms_dch], inc="out")
            emit("sync", lambda s: s.dma_start(out=dbg["s0"], in_=s0_all),
                 waits=[ms_dch], inc="out")
            emit("sync", lambda s: s.dma_start(out=dbg["r0"], in_=dbg_sb[:, 0:48]),
                 waits=[ms_dch], inc="out")
            emit("sync", lambda s: s.dma_start(out=dbg["ch"], in_=dbg_sb[:, 48:112]),
                 waits=[ms_dch], inc="out")
            emit("sync", lambda s: s.dma_start(out=dbg["vk"], in_=vk_sb),
                 waits=[ms_vk], inc="out")
            emit("sync", lambda s: s.dma_start(out=dbg["qT"], in_=qT),
                 waits=[ms_qadd[NDC - 1]], inc="out")
            emit("sync", lambda s: s.dma_start(out=dbg["M"], in_=M_sb),
                 waits=[ms_Mev[NDC - 1]], inc="out")

        final_out = cnt["out"]
        emit("sync", None, waits=[("out", final_out)])

        block = ent(nc.Block())

        def runner(eng_name):
            def run(eng):
                for waits, fn, minc in ops[eng_name]:
                    for (snm, val) in waits:
                        eng.wait_ge(sems[snm], val)
                    if fn is None:
                        continue
                    inst = fn(eng)
                    if minc is not None:
                        snm, val = minc
                        step = 16 if snm in DMA_SEMS else 1
                        inst.then_inc(sems[snm], step)
            return run

        block.tensor(runner("pe"))
        block.vector(runner("dve"))
        block.scalar(runner("act"))
        block.gpsimd(runner("pool"))
        block.sync(runner("sync"))

    return nc


_NC_CACHE = {}


def get_nc(debug=False):
    key = (debug,)
    if key not in _NC_CACHE:
        _NC_CACHE[key] = build_nc(debug)
    return _NC_CACHE[key]


def _prep_w(w):
    return np.ascontiguousarray(
        np.asarray(w, np.float32).reshape(8, 128, 1024)
        .transpose(1, 0, 2).reshape(128, 8192))


def make_in_maps(inputs):
    x = np.ascontiguousarray(np.asarray(inputs["L0_attn_x"], dtype=np.float32))
    cst = np.zeros((128, 280), np.float32)
    cst[:, 0:128] = np.triu(np.ones((128, 128), np.float32))
    cc = np.arange(16)
    cst[:, 128:256] = np.kron((cc[:, None] < cc[None, :]).astype(np.float32),
                              np.eye(8, dtype=np.float32))
    cst[0:8, 256:264] = np.eye(8, dtype=np.float32)
    cst[:, 264:272] = np.arange(8, dtype=np.float32)[None, :]
    cst[0:8, 272] = np.array(BASE, np.float32) - 1.0
    cst[0:8, 273] = np.array([c * 128 for c in CAPT], np.float32)
    cst[0:8, 274] = np.arange(8, dtype=np.float32)
    cst[:, 275] = np.arange(128, dtype=np.float32)
    cst[:, 276] = 1.0

    common = {
        "wq": _prep_w(inputs["L0_attn_Wq"]),
        "wk": _prep_w(inputs["L0_attn_Wk"]),
        "wv": _prep_w(inputs["L0_attn_Wv"]),
        "wp": _prep_w(inputs["L0_attn_Wp"]),
        "bq": np.ascontiguousarray(np.asarray(inputs["L0_attn_bq"], np.float32).reshape(1, D)),
        "gw": np.ascontiguousarray(np.asarray(inputs["L0_ffn_gate_w"], np.float32)),
        "smp": None,
        "gb": np.ascontiguousarray(np.asarray(inputs["L0_ffn_gate_b"], np.float32).reshape(1, 10)),
        "ew": np.ascontiguousarray(
            np.asarray(inputs["L0_ffn_expert_w"], np.float32)
            .reshape(8, 8, 128, 1024).transpose(0, 2, 1, 3).reshape(1024, 8192)),
        "cst": cst,
    }
    smp = np.zeros((65, D), np.float32)
    smp[0:8] = np.asarray(inputs["L0_ffn_expert_b"], np.float32)
    smp[8] = np.asarray(inputs["L0_attn_bp"], np.float32).reshape(D)
    smp[32] = np.asarray(inputs["L0_attn_bk"], np.float32).reshape(D)
    smp[64] = np.asarray(inputs["L0_attn_bv"], np.float32).reshape(D)
    common["smp"] = smp
    in_maps = []
    for c in range(8):
        m = dict(common)
        m["x"] = np.ascontiguousarray(x[c])
        in_maps.append(m)
    return in_maps


def kernel(**inputs):
    nc = get_nc(False)
    in_maps = make_in_maps(inputs)
    res = run_bass_kernel_spmd(nc, in_maps, core_ids=list(range(8)))
    out = np.stack([np.asarray(r["y"]) for r in res.results], axis=0)
    return out


# version 16
# speedup vs baseline: 1.7088x; 1.7088x over previous
import numpy as np
from contextlib import ExitStack

import concourse.bass as bass
import concourse.mybir as mybir
from concourse.bass import IndirectOffsetOnAxis as IOA
from concourse.bass_utils import run_bass_kernel_spmd

F32 = mybir.dt.float32
BF16 = mybir.dt.bfloat16
I32 = mybir.dt.int32
AF = mybir.ActivationFunctionType
ALU = mybir.AluOpType
AX = mybir.AxisListType

P = 128
T = 2048
D = 1024
NC = 16
NDC = 8
NH = 16
HD = 64
NE = 8
EPS = 1e-5
BIG = float(1 << 15)

CAPT = [14, 11, 9, 8, 7, 6, 5, 4]
KBASE = [0]
for c in CAPT[:-1]:
    KBASE.append(KBASE[-1] + c)
NTIL = sum(CAPT)
CAPSUM = NTIL * 128
BASE = [k * 128 for k in KBASE]

DMA_SEMS = ("x0", "x1", "w0", "w1", "w2", "init", "initg", "wpm", "sm",
            "out", "xg", "sc", "wg", "xw", "ys", "ga")


def build_nc(debug=False):
    nc = bass.Bass("TRN2", target_bir_lowering=False,
                   detect_race_conditions=False)

    x_d = nc.dram_tensor("x", [T, D], F32, kind="ExternalInput").ap()
    wq_d = nc.dram_tensor("wq", [P, NDC * D], F32, kind="ExternalInput").ap()
    wk_d = nc.dram_tensor("wk", [P, NDC * D], F32, kind="ExternalInput").ap()
    wv_d = nc.dram_tensor("wv", [P, NDC * D], F32, kind="ExternalInput").ap()
    wp_d = nc.dram_tensor("wp", [P, NDC * D], F32, kind="ExternalInput").ap()
    bq_d = nc.dram_tensor("bq", [1, D], F32, kind="ExternalInput").ap()
    gw_d = nc.dram_tensor("gw", [10, D], F32, kind="ExternalInput").ap()
    sm_d = nc.dram_tensor("smp", [65, D], F32, kind="ExternalInput").ap()
    gb_d = nc.dram_tensor("gb", [1, 10], F32, kind="ExternalInput").ap()
    ew_d = nc.dram_tensor("ew", [NE * P, NDC * D], F32, kind="ExternalInput").ap()
    cst_d = nc.dram_tensor("cst", [P, 280], F32, kind="ExternalInput").ap()
    y_d = nc.dram_tensor("y", [T, D], F32, kind="ExternalOutput").ap()
    kind = "ExternalOutput" if debug else "Internal"
    xng_d = nc.dram_tensor("xng", [T, D], BF16, kind=kind).ap()
    xs_d = nc.dram_tensor("xsort", [CAPSUM, D], BF16, kind=kind).ap()
    yv_d = nc.dram_tensor("yvals", [CAPSUM, D], F32, kind=kind).ap()
    ws_d = nc.dram_tensor("wsort", [CAPSUM, 1], F32, kind=kind).ap()
    xs_r = xs_d.rearrange("(p k) g -> p k g", p=P)
    yv_r = yv_d.rearrange("(p k) g -> p k g", p=P)
    dbg = {}
    if debug:
        dbg["rank"] = nc.dram_tensor("dbg_rank", [P, 128], F32, kind="ExternalOutput").ap()
        dbg["red"] = nc.dram_tensor("dbg_red", [P, 128], F32, kind="ExternalOutput").ap()
        dbg["d0i"] = nc.dram_tensor("dbg_d0i", [P, NC], I32, kind="ExternalOutput").ap()
        dbg["d1i"] = nc.dram_tensor("dbg_d1i", [P, NC], I32, kind="ExternalOutput").ap()
        dbg["widx"] = nc.dram_tensor("dbg_widx", [P, 8], I32, kind="ExternalOutput").ap()
        dbg["srow"] = nc.dram_tensor("dbg_srow", [8, 384], F32, kind="ExternalOutput").ap()
        dbg["msk"] = nc.dram_tensor("dbg_msk", [P, 128], F32, kind="ExternalOutput").ap()
        dbg["s0"] = nc.dram_tensor("dbg_s0", [P, 128], F32, kind="ExternalOutput").ap()
        dbg["r0"] = nc.dram_tensor("dbg_r0", [P, 48], F32, kind="ExternalOutput").ap()
        dbg["ch"] = nc.dram_tensor("dbg_ch", [P, 64], F32, kind="ExternalOutput").ap()
        dbg["vk"] = nc.dram_tensor("dbg_vk", [P, 512], BF16, kind="ExternalOutput").ap()
        dbg["qT"] = nc.dram_tensor("dbg_qT", [P, NDC * T], BF16, kind="ExternalOutput").ap()
        dbg["M"] = nc.dram_tensor("dbg_M", [P, NDC * D], BF16, kind="ExternalOutput").ap()

    with ExitStack() as ctx:
        ent = ctx.enter_context

        xnT_h = ent(nc.sbuf_tensor("xnT", [P, NDC * T], BF16))
        xnT = xnT_h[:]
        xnT_f32 = xnT_h[:].bitcast(F32)
        qT = ent(nc.sbuf_tensor("qT", [P, NDC * T], BF16))[:]
        M_sb = ent(nc.sbuf_tensor("M_sb", [P, NDC * D], BF16))[:]
        W_sb = [ent(nc.sbuf_tensor(f"W{b}", [P, NDC * D], BF16))[:] for b in range(3)]
        acc_h = ent(nc.sbuf_tensor("acc", [P, 8 * D], F32))
        acc = acc_h[:]
        acc_bf = acc_h[:].bitcast(BF16)
        x_pp = [ent(nc.sbuf_tensor(f"xpp{b}", [P, D], F32))[:] for b in range(2)]
        xn_bf = [ent(nc.sbuf_tensor(f"xnbf{b}", [P, D], BF16))[:] for b in range(2)]
        ws_sb = ent(nc.sbuf_tensor("ws_sb", [P, NTIL], F32))[:]
        xw_pp = [xnT[:, 0:D], xnT[:, D:2 * D]]
        xg_pp = [xnT[:, 2 * D:3 * D], xnT[:, 3 * D:4 * D]]
        yst = [xnT_f32[:, 2048:3072], xnT_f32[:, 3072:4096]]
        gw_bf = ent(nc.sbuf_tensor("gw_bf", [10, D], BF16))[:]
        bq8_bf = ent(nc.sbuf_tensor("bq8_bf", [NDC, P], BF16))[:]
        vk_sb = ent(nc.sbuf_tensor("vk_sb", [P, 512], BF16))[:]
        cb_sb = ent(nc.sbuf_tensor("cb_sb", [9, T], BF16))[:]
        c_all = ent(nc.sbuf_tensor("c_all", [P, NC * NE], F32))[:]
        c_bf = ent(nc.sbuf_tensor("c_bf", [P, NC * NE], BF16))[:]
        msk_all = ent(nc.sbuf_tensor("msk_all", [P, NC * NE], F32))[:]
        s0_all = ent(nc.sbuf_tensor("s0_all", [P, NC * NE], F32))[:]
        ib = ent(nc.sbuf_tensor("ib", [P, 880], F32))[:]
        dbg_sb = ent(nc.sbuf_tensor("dbg_sb", [P, 112], F32))[:] if debug else None
        rw_sb = ent(nc.sbuf_tensor("rw_sb", [P, 32], F32))[:]
        d0i = ent(nc.sbuf_tensor("d0i", [P, NC], I32))[:]
        d1i = ent(nc.sbuf_tensor("d1i", [P, NC], I32))[:]
        widx = ent(nc.sbuf_tensor("widx", [P, 8], I32))[:]
        srow = ent(nc.sbuf_tensor("srow", [8, 384], F32))[:]
        tcol = ent(nc.sbuf_tensor("tcol", [P, 1], F32))[:]
        cst_sb = ent(nc.sbuf_tensor("cst_sb", [P, 280], F32))[:]
        gwT = qT[:, 0:80]
        ones = qT[0:65, 80:208]
        zero = qT[0:1, 208:720]
        bqT = ent(nc.sbuf_tensor("bqT", [P, NDC], F32))[:]
        sm_sb = ent(nc.sbuf_tensor("sm_sb", [65, D], BF16))[:]
        gb_bf = ent(nc.sbuf_tensor("gb_bf", [1, 10], BF16))[:]
        eb_sb = sm_sb[0:9, :]
        bk_bf = sm_sb[32:33, :]
        bv_bf = sm_sb[64:65, :]
        ident = ent(nc.sbuf_tensor("ident", [P, P], BF16))[:]
        cst_eps = ent(nc.sbuf_tensor("cst_eps", [P, 1], F32))[:]
        nc.const_aps.register(F32, EPS, cst_eps)
        egt = [ib[:, 0:10], ib[:, 10:20]]
        pt = [ib[:, 20:30], ib[:, 30:40]]
        m8t = [ib[:, 40:48], ib[:, 48:56]]
        wtst = [ib[:, 56:64], ib[:, 64:72]]
        sm = {}
        for ni, nm in enumerate(("sum", "mean", "ssq", "ssqn", "nvar", "lnv",
                                 "rstd", "m1", "nm1", "Z", "rz", "den", "ds",
                                 "rd")):
            sm[nm] = [ib[:, 72 + 2 * ni + b:73 + 2 * ni + b] for b in range(2)]

        cL = cst_sb[:, 0:128]
        cPM = cst_sb[:, 128:256]
        cIdc = cst_sb[0:8, 256:264]
        cErow = cst_sb[0:8, 264:272]
        cBasem1 = cst_sb[0:8, 272:273]
        cCap = cst_sb[0:8, 273:274]
        cEcol = cst_sb[0:8, 274:275]
        cPcol = cst_sb[:, 275:276]
        cOnecol = cst_sb[:, 276:277]

        sTot = srow[0:1, 0:128]
        sCarry = srow[0:1, 128:256]
        sLoads = srow[0:1, 256:264]
        sKeyrow = srow[0:1, 264:272]
        sSigrow = srow[0:1, 272:280]
        sRankrow = srow[0:1, 280:288]
        sKeyT = srow[0:8, 288:289]
        sRankT = srow[0:8, 289:290]
        sM2 = srow[0:8, 292:300]
        sG2T = srow[0:8, 300:308]
        sSigcol = srow[0:8, 308:309]
        sBrow = srow[0:1, 312:320]
        sCrow = srow[0:1, 320:328]
        sTmp8 = srow[0:8, 328:336]

        ibRank = ib[:, 0:128]
        ibBrep = ib[:, 128:256]
        ibCrep = ib[:, 256:384]
        ibT1 = ib[:, 384:512]
        ibBad = ib[:, 512:640]
        ibPr = ib[:, 640:768]
        rR = [ib[:, 768:784]] * 2
        rAny = [ib[:, 784:800]] * 2
        rBad = [ib[:, 800:816]] * 2
        mR = ib[:, 816:832]
        mAny = ib[:, 832:848]
        mBad = ib[:, 848:864]
        mW = ib[:, 864:880]
        iPm = ib[:, 816:832]
        iKm = ib[:, 832:848]
        iK2 = ib[:, 848:864]
        iRp = ib[:, 832:848]
        iU = ib[:, 864:880]
        iT4 = ib[:, 816:832]
        iRpf = ib[:, 864:880]
        widxf = ib[:, 816:824]
        rW = [rw_sb[:, 0:16], rw_sb[:, 16:32]]

        sq_thr = acc[:, 0:D]
        xm_scr = acc[:, D:2 * D]
        def abf(c0, n):
            return acc_bf[:, c0:c0 + n]
        escr = [abf(4096 + 1024 * b, D) for b in range(2)]
        rscr = [abf(6144 + 1024 * b, D) for b in range(2)]
        mscr = [abf(8192 + 1024 * b, D) for b in range(2)]
        kscr = [abf(10240 + 1024 * b, D) for b in range(2)]
        vscr = [abf(12288 + 1024 * b, D) for b in range(2)]
        escrB = [abf(4096 + 2048 * b, 2048) for b in range(2)]
        rscrB = [abf(8192 + 2048 * b, 2048) for b in range(2)]
        mscrB = [abf(12288 + 2048 * b, 2048) for b in range(2)]
        acc_t = [acc[:, D * il:D * il + D] for il in range(4)]

        P2a = ent(nc.psum_tensor("P2a", [P, D], F32))[:]
        P2b = ent(nc.psum_tensor("P2b", [P, D], F32))[:]
        P2c_h = ent(nc.psum_tensor("P2c", [P, D], F32))
        P2c = P2c_h[:]
        P2c_bf = P2c_h[:].bitcast(BF16)
        P1a = ent(nc.psum_tensor("P1a", [P, 512], F32))[:]
        P1b_h = ent(nc.psum_tensor("P1b", [P, 512], F32))
        P1b = P1b_h[:]
        P1b_bf = P1b_h[:].bitcast(BF16)

        sems = {}
        for nm in ("pe", "dve", "act", "pool") + DMA_SEMS:
            sems[nm] = ent(nc.semaphore(f"s_{nm}"))
        sems["dvex"] = ent(nc.semaphore("s_dvex"))
        sems["actx"] = ent(nc.semaphore("s_actx"))
        _cutc = {"dvex": 0, "actx": 0}

        def mkcut(semname):
            def cut(eng, inst):
                _cutc[semname] += 1
                inst.then_inc(sems[semname], 1)
                eng.wait_ge(sems[semname], _cutc[semname])
            return cut
        vcut = mkcut("dvex")
        acut = mkcut("actx")

        ops = {"pe": [], "dve": [], "act": [], "pool": [], "sync": []}
        cnt = {nm: 0 for nm in ("pe", "dve", "act", "pool") + DMA_SEMS}

        def emit(eng, fn, waits=(), inc=None):
            m = None
            if inc is not None:
                cnt[inc] += 16 if inc in DMA_SEMS else 1
                m = (inc, cnt[inc])
            ops[eng].append((tuple(waits), fn, m))
            return m

        MM = dict(skip_group_check=True)

        regs = {}

        def f_regs(g):
            regs["bcap"] = g.to_reg(CAPSUM - 1)
            regs["bew"] = g.to_reg(NE * P - 1)
            return None
        emit("pool", f_regs)

        def f_init_pool0(g):
            g.memset(ones, 1.0)
            g.memset(zero, 0.0)
            g.memset(cst_eps, EPS)
            g.memset(cb_sb, 1.0)
            return g.memset(ident, 0.0)
        ms_initp0 = emit("pool", f_init_pool0, inc="pool")

        def f_init_pool(g):
            return g.affine_select(
                out=ident, in_=ident, compare_op=ALU.not_equal, fill=1.0,
                base=0, pattern=[[-1, P]], channel_multiplier=1)
        ms_initp = emit("pool", f_init_pool, waits=[ms_initp0], inc="pool")

        ms_cstl = emit("sync", lambda s: s.dma_start(out=cst_sb, in_=cst_d),
                       inc="init")

        ms_gw = emit("pool", lambda g: g.dma_start(out=gw_bf, in_=gw_d),
                     inc="initg")
        emit("pool", lambda g: g.dma_start(
            out=bq8_bf, in_=bq_d.rearrange("o (c p) -> (o c) p", p=P)),
            inc="initg")
        ms_initg2 = ("initg", cnt["initg"])
        ms_wk = emit("pool", lambda g: g.dma_start(out=W_sb[0], in_=wk_d),
                     inc="w0")
        ms_wv = emit("pool", lambda g: g.dma_start(out=W_sb[1], in_=wv_d),
                     inc="w1")
        ms_sm = emit("pool", lambda g: g.dma_start(out=sm_sb, in_=sm_d),
                     inc="sm")
        emit("pool", lambda g: g.dma_start(out=gb_bf, in_=gb_d), inc="init")
        init_all = ("init", cnt["init"])
        ms_wq = emit("pool", lambda g: g.dma_start(out=W_sb[2], in_=wq_d),
                     inc="w2")
        ms_wpm = emit("pool", lambda g: g.dma_start(out=M_sb, in_=wp_d),
                      inc="wpm")

        def f_gwT(t):
            last = None
            for c in range(NDC):
                last = t.matmul(P1b_bf[:, 10 * c:10 * c + 10],
                                gw_bf[0:10, P * c:P * c + P], ident[0:10, 0:10],
                                is_transpose=True, start=True, stop=True, **MM)
            return last
        def f_bqT(t):
            return t.matmul(P1b_bf[:, 128:128 + NDC], bq8_bf,
                            ident[0:NDC, 0:NDC],
                            is_transpose=True, start=True, stop=True, **MM)

        def emit_gwT():
            pe = emit("pe", f_gwT, waits=[ms_gw, ms_initp], inc="pe")
            pe2 = emit("pe", f_bqT, waits=[ms_initg2], inc="pe")
            emit("dve", lambda v: v.tensor_copy(bqT, P1b_bf[:, 128:128 + NDC]),
                 waits=[pe2], inc="dve")
            return emit("dve", lambda v: v.tensor_copy(gwT, P1b_bf[:, 0:80]),
                        waits=[pe], inc="dve")

        ms_xload = {}
        ms_xn = {}
        ms_sq = {}
        ms_trPE = {}
        ms_trEvac = {}
        ms_kpe = {}
        ms_vpe = {}
        ms_gpe = {}
        ms_ek = {}
        ms_rk = {}
        ms_mk = {}
        ms_kt = {}
        ms_vcp = {}
        ms_kvpe = {}
        ms_gd1 = {}
        ms_eg = {}
        ms_gd2 = {}
        ms_xg = {}

        def xnT_sl(d, ii):
            return xnT[:, T * d + P * ii: T * d + P * ii + P]

        def e_front(i):
            b = i % 2
            w = []
            if i >= 2:
                w = [ms_xn[i - 2], ms_sq[i - 2]]
            ms_xload[i] = emit(
                "sync",
                (lambda ii, bb: (lambda s: s.dma_start(
                    out=x_pp[bb], in_=x_d[P * ii:P * ii + P, :])))(i, b),
                waits=w, inc=f"x{b}")

            def f_ln1(v, bb=b):
                vcut(v, v.reduce_sum(sm["sum"][bb], x_pp[bb], AX.X))
                return v.tensor_scalar_mul(sm["mean"][bb], sm["sum"][bb], 1.0 / D)
            emit("dve", f_ln1, waits=[ms_xload[i]])

            ms_sq[i] = emit(
                "act",
                (lambda bb: (lambda s: s.activation(
                    sq_thr, x_pp[bb], AF.Square, accum_out=sm["ssq"][bb])))(b),
                waits=[ms_xload[i]], inc="act")

            def f_ln2(v, bb=b):
                vcut(v, v.tensor_scalar_mul(sm["ssqn"][bb], sm["ssq"][bb], 1.0 / D))
                return v.scalar_tensor_tensor(
                    sm["nvar"][bb], sm["mean"][bb], sm["mean"][bb], sm["ssqn"][bb],
                    ALU.mult, ALU.subtract)
            ms_nvar = emit("dve", f_ln2, waits=[ms_sq[i]], inc="dve")

            def f_rstd(s, bb=b):
                acut(s, s.activation(sm["lnv"][bb], sm["nvar"][bb], AF.Ln,
                                     scale=-1.0, bias=EPS))
                return s.activation(sm["rstd"][bb], sm["lnv"][bb], AF.Exp,
                                    scale=-0.5)
            w = [ms_nvar]
            if i == 0:
                w.append(ms_initp)
            ms_rstd = emit("act", f_rstd, waits=w, inc="act")

            def f_norm(v, bb=b):
                v.tensor_scalar_sub(xm_scr, x_pp[bb], sm["mean"][bb])
                return v.tensor_scalar_mul(xn_bf[bb], xm_scr, sm["rstd"][bb])
            w = [ms_rstd]
            if i >= 2:
                w.append(ms_trPE[i - 2])
                w.append(ms_xg[i - 2])
            ms_xn[i] = emit("dve", f_norm, waits=w, inc="dve")

            ms_xg[i] = emit(
                "act",
                (lambda ii, bb: (lambda s: s.dma_start(
                    out=xng_d[P * ii:P * ii + P, :], in_=xn_bf[bb])))(i, b),
                waits=[ms_xn[i]], inc="xg")

        def e_tr(i):
            b = i % 2
            tr_bank = P2c_bf[:, 1024 * b:1024 * b + 1024]

            def f_tr(t, bb=b, trb=tr_bank):
                last = None
                for c in range(NDC):
                    last = t.matmul(trb[:, P * c:P * c + P],
                                    xn_bf[bb][:, P * c:P * c + P], ident,
                                    is_transpose=True, start=True, stop=True, **MM)
                return last
            w = [ms_xn[i]]
            if i >= 2:
                w.append(ms_trEvac[i - 2])
            ms_trPE[i] = emit("pe", f_tr, waits=w, inc="pe")

            xnT_dst = xnT.rearrange("p (d t) -> p d t", t=T)[:, :, P * i:P * i + P]

            def f_trev(v, trb=tr_bank, dst=xnT_dst):
                src_ = trb.rearrange("p (d t) -> p d t", t=P)
                return v.tensor_copy(dst, src_)
            ms_trEvac[i] = emit("dve", f_trev, waits=[ms_trPE[i]], inc="dve")

        def e_kv(i):
            b = i % 2

            def f_kv(t, ii=i, bb=b):
                last = None
                if ii == 0:
                    t.matmul(P1a, ones[0:1, 0:P], zero[0:1, 0:512],
                             start=True, stop=False, **MM)
                for h in range(NH):
                    pr = h // 2
                    par = h % 2
                    out = P1a[64 * par:64 * par + 64, 64 * pr:64 * pr + 64]
                    last = t.matmul(out, vscr[bb][:, HD * h:HD * h + HD],
                                    kscr[bb][:, HD * h:HD * h + HD],
                                    start=False,
                                    stop=(ii == NC - 1 and h == NH - 1), **MM)
                return last
            ms_kvpe[i] = emit("pe", f_kv, waits=[ms_kt[i], ms_vcp[i]], inc="pe")

        def e_mms(i):
            b = i % 2

            def f_g(t, ii=i):
                t.matmul(P1b[:, 0:10], ones[0:1, 0:P], gb_bf[0:1, 0:10],
                         start=True, stop=False, **MM)
                last = None
                for d in range(NDC):
                    last = t.matmul(P1b[:, 0:10], xnT_sl(d, ii),
                                    gwT[:, 10 * d:10 * d + 10],
                                    start=False, stop=d == NDC - 1, **MM)
                return last

            def f_k(t, ii=i):
                last = None
                for d in range(NDC):
                    t.matmul(P2a[:, 0:512], xnT_sl(d, ii),
                             W_sb[0][:, D * d:D * d + 512],
                             start=d == 0, stop=False, **MM)
                    last = t.matmul(P2a[:, 512:1024], xnT_sl(d, ii),
                                    W_sb[0][:, D * d + 512:D * d + 1024],
                                    start=d == 0, stop=False, **MM)
                return last

            def f_kb(t):
                t.matmul(P2a[:, 0:512], ones[32:33, 0:P], bk_bf[0:1, 0:512],
                         start=False, stop=True, **MM)
                return t.matmul(P2a[:, 512:1024], ones[32:33, 0:P],
                                bk_bf[0:1, 512:1024],
                                start=False, stop=True, **MM)

            def f_v(t, ii=i):
                last = None
                for d in range(NDC):
                    t.matmul(P2b[:, 0:512], xnT_sl(d, ii),
                             W_sb[1][:, D * d:D * d + 512],
                             start=d == 0, stop=False, **MM)
                    last = t.matmul(P2b[:, 512:1024], xnT_sl(d, ii),
                                    W_sb[1][:, D * d + 512:D * d + 1024],
                                    start=d == 0, stop=False, **MM)
                return last

            def f_vb(t):
                t.matmul(P2b[:, 0:512], ones[64:65, 0:P], bv_bf[0:1, 0:512],
                         start=False, stop=True, **MM)
                return t.matmul(P2b[:, 512:1024], ones[64:65, 0:P],
                                bv_bf[0:1, 512:1024],
                                start=False, stop=True, **MM)

            w = [ms_trEvac[i]]
            if i == 0:
                w += [ms_wk]
            else:
                w += [ms_rk[i - 1]]
            emit("pe", f_k, waits=w, inc="pe")
            ms_kpe[i] = emit("pe", f_kb,
                             waits=[ms_sm] if i == 0 else (), inc="pe")
            w = [ms_trEvac[i]]
            if i == 0:
                w.append(ms_wv)
            else:
                w.append(ms_vcp[i - 1])
            emit("pe", f_v, waits=w, inc="pe")
            ms_vpe[i] = emit("pe", f_vb,
                             waits=[ms_sm] if i == 0 else (), inc="pe")
            w = [ms_trEvac[i]]
            if i == 0:
                w += [ms_gwT, init_all]
            else:
                w += [ms_eg[i - 1], ms_gd1[i - 1]]
            ms_gpe[i] = emit("pe", f_g, waits=w, inc="pe")

        def e_evacs(i):
            b = i % 2
            ms_ek[i] = emit(
                "act",
                (lambda bb: (lambda s: s.activation(escr[bb], P2a, AF.Exp)))(b),
                waits=[ms_kpe[i]] + ([ms_mk[i - 2]] if i >= 2 else []),
                inc="act")
            ms_rk[i] = emit(
                "act",
                (lambda bb: (lambda s: s.activation(rscr[bb], P2a, AF.Relu)))(b),
                waits=[ms_kpe[i]], inc="act")
            ms_mk[i] = emit(
                "dve",
                (lambda bb: (lambda v: v.tensor_scalar_min(mscr[bb], escr[bb], 1.0)))(b),
                waits=[ms_ek[i]] + ([ms_kt[i - 2]] if i >= 2 else []),
                inc="dve")
            w = [ms_mk[i], ms_rk[i]]
            if i >= 2:
                w.append(ms_kvpe[i - 2])
            ms_kt[i] = emit(
                "dve",
                (lambda bb: (lambda v: v.tensor_tensor(
                    kscr[bb], rscr[bb], mscr[bb], ALU.add)))(b),
                waits=w, inc="dve")
            w = [ms_vpe[i]]
            if i >= 2:
                w.append(ms_kvpe[i - 2])
            ms_vcp[i] = emit(
                "act",
                (lambda bb: (lambda s: s.copy(vscr[bb], P2b)))(b),
                waits=w, inc="act")

        def e_gate(i):
            b = i % 2

            def f_gd1(v, bb=b):
                vcut(v, v.reduce_max(sm["m1"][bb], P1b[:, 0:10], AX.X))
                return v.tensor_scalar_mul(sm["nm1"][bb], sm["m1"][bb], -1.0)
            ms_gd1[i] = emit("dve", f_gd1, waits=[ms_gpe[i]], inc="dve")

            w = [ms_gd1[i]]
            if i >= 2:
                w.append(ms_gd2[i - 2])
            ms_eg[i] = emit(
                "act",
                (lambda bb: (lambda s: s.activation(
                    egt[bb], P1b[:, 0:10], AF.Exp, bias=sm["nm1"][bb])))(b),
                waits=w, inc="act")

            def f_gd2(v, bb=b, ii=i):
                vcut(v, v.reduce_sum(sm["Z"][bb], egt[bb], AX.X))
                vcut(v, v.reciprocal(sm["rz"][bb], sm["Z"][bb]))
                vcut(v, v.tensor_scalar_mul(pt[bb], egt[bb], sm["rz"][bb]))
                vcut(v, v.max(m8t[bb], pt[bb]))
                vcut(v, v.tensor_single_scalar(msk_all[:, 8 * ii:8 * ii + 8],
                                               pt[bb][:, 0:8],
                                               m8t[bb][:, 1:2], ALU.is_ge))
                vcut(v, v.tensor_single_scalar(s0_all[:, 8 * ii:8 * ii + 8],
                                               pt[bb][:, 0:8],
                                               m8t[bb][:, 0:1], ALU.is_ge))
                vcut(v, v.tensor_tensor(wtst[bb], msk_all[:, 8 * ii:8 * ii + 8],
                                        pt[bb][:, 0:8], ALU.mult))
                vcut(v, v.reduce_sum(sm["den"][bb], wtst[bb], AX.X))
                vcut(v, v.tensor_scalar_max(sm["ds"][bb], sm["den"][bb], 1e-30))
                vcut(v, v.reciprocal(sm["rd"][bb], sm["ds"][bb]))
                vcut(v, v.tensor_scalar_mul(c_all[:, 8 * ii:8 * ii + 8], wtst[bb],
                                            sm["rd"][bb]))
                return v.tensor_copy(c_bf[:, 8 * ii:8 * ii + 8],
                                     c_all[:, 8 * ii:8 * ii + 8])
            ms_gd2[i] = emit("dve", f_gd2, waits=[ms_eg[i]], inc="dve")

        e_front(0)
        e_front(1)
        e_tr(0)
        ms_gwT = emit_gwT()
        for i in range(NC):
            if i + 2 < NC:
                e_front(i + 2)
            if i >= 1:
                e_kv(i - 1)
            if i + 1 < NC:
                e_tr(i + 1)
            e_mms(i)
            e_evacs(i)
            e_gate(i)
        e_kv(NC - 1)

        def f_ct(t):
            last = None
            for i in range(NC):
                last = t.matmul(P2c_bf[0:8, P * i:P * i + P],
                                c_bf[:, 8 * i:8 * i + 8], ident,
                                is_transpose=True, start=True, stop=True, **MM)
            return last
        ms_ctpe = emit("pe", f_ct,
                       waits=[ms_gd2[NC - 1], ms_trEvac[NC - 1],
                              ms_trEvac[NC - 2]],
                       inc="pe")
        ms_cb = emit("dve", lambda v: v.tensor_copy(cb_sb[0:8, :], P2c_bf[0:8, :]),
                     waits=[ms_ctpe, ms_initp0], inc="dve")
        ms_vk = emit("dve", lambda v: v.tensor_copy(vk_sb, P1a),
                     waits=[ms_kvpe[NC - 1]], inc="dve")

        ms_Mpe = {}
        ms_Mev = {}
        for pr in range(NDC):
            ps = P2a if pr % 2 == 0 else P2b

            def f_M(t, pp=pr, ps_=ps):
                last = None
                for par in range(2):
                    r0 = 64 * par
                    for hh in range(2):
                        last = t.matmul(
                            ps_[r0:r0 + 64, 512 * hh:512 * hh + 512],
                            vk_sb[r0:r0 + 64, 64 * pp:64 * pp + 64],
                            M_sb[r0:r0 + 64, D * pp + 512 * hh:D * pp + 512 * hh + 512],
                            start=True, stop=True, **MM)
                return last
            if pr == 0:
                w = [ms_vk, ms_wpm, ms_rk[NC - 1]]
            elif pr == 1:
                w = [ms_vcp[NC - 1]]
            else:
                w = [ms_Mev[pr - 2]]
            ms_Mpe[pr] = emit("pe", f_M, waits=w, inc="pe")
            ms_Mev[pr] = emit(
                "dve",
                (lambda pp, ps_: (lambda v: v.tensor_copy(
                    M_sb[:, D * pp:D * pp + D], ps_)))(pr, ps),
                waits=[ms_Mpe[pr]], inc="dve")

        ms_i1 = emit("pe", lambda t: t.matmul(
            P2c[0:1, 0:128], cOnecol, msk_all, start=True, stop=True, **MM),
            waits=[ms_cb, ms_gd2[NC - 1], ms_cstl], inc="pe")
        ms_i2 = emit("dve", lambda v: v.tensor_copy(sTot, P2c[0:1, 0:128]),
                     waits=[ms_i1], inc="dve")
        ms_i3 = emit("pe", lambda t: t.matmul(
            P2c[:, 128:129], sTot, cIdc[0:1, 0:1],
            is_transpose=True, start=True, stop=True, **MM),
            waits=[ms_i2], inc="pe")
        ms_i4 = emit("dve", lambda v: v.tensor_copy(tcol, P2c[:, 128:129]),
                     waits=[ms_i3], inc="dve")
        ms_i5 = emit("pe", lambda t: t.matmul(
            P2c[0:1, 256:384], tcol, cPM, start=True, stop=True, **MM),
            waits=[ms_i4], inc="pe")

        def f_i6(v):
            vcut(v, v.tensor_copy(sCarry, P2c[0:1, 256:384]))
            vcut(v, v.tensor_tensor(sLoads, sCarry[0:1, 120:128],
                                    sTot[0:1, 120:128], ALU.add))
            vcut(v, v.tensor_scalar_mul(sKeyrow, sLoads, 8.0))
            return v.tensor_tensor(sKeyrow, sKeyrow, cErow[0:1, 0:8], ALU.add)
        ms_i6 = emit("dve", f_i6, waits=[ms_i5], inc="dve")

        def f_rank(t):
            last = None
            for c in range(NC):
                t.matmul(P1a[:, 8 * c:8 * c + 8], cL[0:1, 0:128],
                         sCarry[0:1, 8 * c:8 * c + 8],
                         start=True, stop=False, **MM)
                last = t.matmul(P1a[:, 8 * c:8 * c + 8], cL,
                                msk_all[:, 8 * c:8 * c + 8],
                                start=False, stop=True, **MM)
            return last
        ms_i7 = emit("pe", f_rank, waits=[ms_i6, ms_vk], inc="pe")

        ms_i8 = emit("pe", lambda t: t.matmul(
            P2c[0:8, 132:133], sKeyrow, cIdc[0:1, 0:1],
            is_transpose=True, start=True, stop=True, **MM),
            waits=[ms_i6], inc="pe")
        ms_i9 = emit("pe", lambda t: t.matmul(
            P2c[0:8, 136:144], cL[0:1, 0:8], sKeyrow,
            start=True, stop=True, **MM),
            waits=[ms_i6], inc="pe")

        def f_i10(v):
            vcut(v, v.tensor_copy(sKeyT, P2c[0:8, 132:133]))
            vcut(v, v.tensor_single_scalar(sM2, P2c[0:8, 136:144], sKeyT,
                                           ALU.is_gt))
            return v.reduce_sum(sRankT, sM2, AX.X)
        ms_i10 = emit("dve", f_i10, waits=[ms_i8, ms_i9], inc="dve")

        ms_i11 = emit("pe", lambda t: t.matmul(
            P2c[0:1, 160:168], sRankT, cIdc, is_transpose=True,
            start=True, stop=True, **MM), waits=[ms_i10], inc="pe")
        ms_i12 = emit("dve", lambda v: v.tensor_copy(sRankrow, P2c[0:1, 160:168]),
                      waits=[ms_i11], inc="dve")
        ms_i13 = emit("pe", lambda t: t.matmul(
            P2c[0:8, 168:176], cL[0:1, 0:8], sRankrow,
            start=True, stop=True, **MM), waits=[ms_i12], inc="pe")

        def f_i14(v):
            vcut(v, v.tensor_single_scalar(sG2T, P2c[0:8, 168:176], cEcol,
                                           ALU.is_equal))
            vcut(v, v.tensor_tensor(sTmp8, sG2T, cErow, ALU.mult))
            return v.reduce_sum(sSigcol, sTmp8, AX.X)
        ms_i14 = emit("dve", f_i14, waits=[ms_i13], inc="dve")

        ms_i15 = emit("pe", lambda t: t.matmul(
            P2c[0:1, 176:184], sSigcol, cIdc, is_transpose=True,
            start=True, stop=True, **MM), waits=[ms_i14], inc="pe")
        ms_i16 = emit("dve", lambda v: v.tensor_copy(sSigrow, P2c[0:1, 176:184]),
                      waits=[ms_i15], inc="dve")
        ms_i17 = emit("pe", lambda t: t.matmul(
            P1b[:, 0:8], cL[0:1, 0:128], sSigrow, start=True, stop=True, **MM),
            waits=[ms_i16, ms_gd2[NC - 1], ms_eg[NC - 1]], inc="pe")

        def f_i18(v):
            vcut(v, v.tensor_scalar_mul(widxf, P1b[:, 0:8], 128.0))
            vcut(v, v.tensor_single_scalar(ib[:, 824:832], widxf, cPcol,
                                           ALU.add))
            return v.tensor_copy(widx, ib[:, 824:832])
        ms_widx = emit("dve", f_i18, waits=[ms_i17], inc="dve")

        ms_i19 = emit("pe", lambda t: t.matmul(
            P2c[0:1, 184:192], cBasem1, sG2T, start=True, stop=True, **MM),
            waits=[ms_i14], inc="pe")
        ms_i20 = emit("pe", lambda t: t.matmul(
            P2c[0:1, 192:200], cCap, sG2T, start=True, stop=True, **MM),
            waits=[ms_i14], inc="pe")

        def f_i21(v):
            vcut(v, v.tensor_copy(sBrow, P2c[0:1, 184:192]))
            return v.tensor_copy(sCrow, P2c[0:1, 192:200])
        ms_i21 = emit("dve", f_i21, waits=[ms_i19, ms_i20], inc="dve")

        def f_i22(t):
            last = None
            for c in range(NC):
                t.matmul(P1b[:, 128 + 8 * c:128 + 8 * c + 8],
                         cL[0:1, 0:128], sBrow, start=True, stop=True, **MM)
                last = t.matmul(P1b[:, 256 + 8 * c:256 + 8 * c + 8],
                                cL[0:1, 0:128], sCrow,
                                start=True, stop=True, **MM)
            return last
        ms_i22 = emit("pe", f_i22, waits=[ms_i21], inc="pe")
        ms_i23 = ms_i22

        def f_i24(v):
            v.tensor_copy(ibRank, P1a[:, 0:128])
            v.tensor_copy(ibBrep, P1b[:, 128:256])
            return v.tensor_copy(ibCrep, P1b[:, 256:384])
        ms_idxevac = emit("dve", f_i24, waits=[ms_i7, ms_i22, ms_i23],
                          inc="dve")

        def seg(ap):
            return ap.rearrange("p (k e) -> p k e", e=8)

        def f_redset(v, msks, oR, oAny, oBad, oW):
            vcut(v, v.tensor_tensor(ibPr, msks, ibT1, ALU.mult))
            vcut(v, v.reduce_sum(oR, seg(ibPr), AX.X))
            vcut(v, v.reduce_sum(oAny, seg(msks), AX.X))
            vcut(v, v.tensor_tensor(ibPr, msks, ibBad, ALU.mult))
            vcut(v, v.reduce_sum(oBad, seg(ibPr), AX.X))
            vcut(v, v.tensor_tensor(ibPr, msks, c_all, ALU.mult))
            vcut(v, v.reduce_sum(oW, seg(ibPr), AX.X))

        def f_rpchain(v, s):
            vcut(v, v.tensor_scalar_mul(iPm, rR[s], 1.0 / 128.0))
            vcut(v, v.tensor_scalar_add(iKm, iPm, -0.49609375))
            vcut(v, v.tensor_scalar_add(iK2, iKm, 12582912.0))
            vcut(v, v.tensor_scalar_add(iKm, iK2, -12582912.0))
            vcut(v, v.scalar_tensor_tensor(iPm, iKm, -128.0, rR[s],
                                           ALU.mult, ALU.add))
            vcut(v, v.scalar_tensor_tensor(iK2, iPm, 64.0, iKm,
                                           ALU.mult, ALU.add))
            vcut(v, v.tensor_tensor(iU, rBad[s], rAny[s], ALU.subtract))
            vcut(v, v.scalar_tensor_tensor(iT4, iU, BIG, iK2,
                                           ALU.mult, ALU.add))
            vcut(v, v.tensor_scalar_add(iRpf, iT4, BIG))
            vcut(v, v.tensor_scalar_mul(iKm, iRpf, -1.0))
            vcut(v, v.tensor_tensor(iT4, iKm, iRpf, ALU.max))
            last = v.tensor_copy(d0i if s == 0 else d1i, iT4)
            if s == 0:
                vcut(v, last)
            return last

        def f_dch(v):
            vcut(v, v.tensor_tensor(ibT1, ibRank, ibBrep, ALU.add))
            vcut(v, v.tensor_tensor(ibBad, ibRank, ibCrep, ALU.is_gt))
            f_redset(v, s0_all, rR[0], rAny[0], rBad[0], rW[0])
            if debug:
                vcut(v, v.tensor_copy(dbg_sb[:, 0:48], ib[:, 768:816]))
            f_rpchain(v, 0)
            if debug:
                vcut(v, v.tensor_copy(dbg_sb[:, 48:112], ib[:, 816:880]))
            f_redset(v, msk_all, mR, mAny, mBad, mW)
            vcut(v, v.tensor_tensor(rW[1], mW, rW[0], ALU.subtract))
            vcut(v, v.tensor_tensor(rR[1], mR, rR[0], ALU.subtract))
            vcut(v, v.tensor_tensor(rAny[1], mAny, rAny[0], ALU.subtract))
            vcut(v, v.tensor_tensor(rBad[1], mBad, rBad[0], ALU.subtract))
            return f_rpchain(v, 1)
        ms_dch = emit("dve", f_dch, waits=[ms_idxevac], inc="dve")

        ms_wg = {}
        wg_usage = {0: ms_kpe[NC - 1], 1: ms_vpe[NC - 1]}

        def emit_wg(j):
            w = [ms_widx, wg_usage[j]]
            ms_wg[j] = emit(
                "pool",
                (lambda jj: (lambda g: g.indirect_dma_start(
                    out=W_sb[jj % 3], out_offset=None, in_=ew_d,
                    in_offset=IOA(widx[:, jj:jj + 1], 0),
                    bounds_check=regs["bew"], oob_is_err=False)))(j),
                waits=w, inc="wg")

        emit_wg(0)
        emit_wg(1)

        ms_rl = {}
        ms_sct = {}
        for i in range(NC):
            b = i % 2
            if i < 2:
                w = [ms_xg[NC - 2 + i], ms_trPE[NC - 2 + i]]
            else:
                w = [ms_xg[i], ms_sct[i - 2]]
            ms_rl[i] = emit(
                "sync",
                (lambda ii, bb: (lambda s: s.dma_start(
                    out=xn_bf[bb], in_=xng_d[P * ii:P * ii + P, :])))(i, b),
                waits=w, inc=f"x{b}")
            for s in range(2):
                di = d0i if s == 0 else d1i
                w2 = [ms_rl[i]] + ([ms_dch] if (i == 0 and s == 0) else [])
                ms_sct[i] = emit(
                    "pool",
                    (lambda ii, bb, di_: (lambda g: g.indirect_dma_start(
                        out=xs_d, out_offset=IOA(di_[:, ii:ii + 1], 0),
                        in_=xn_bf[bb], in_offset=None,
                        bounds_check=regs["bcap"], oob_is_err=False)))(i, b, di),
                    waits=w2, inc="sc")
        ms_scall = ("sc", cnt["sc"])
        for i in range(NC):
            for s in range(2):
                di = d0i if s == 0 else d1i
                emit("pool",
                     (lambda ii, ss, di_: (lambda g: g.indirect_dma_start(
                         out=ws_d, out_offset=IOA(di_[:, ii:ii + 1], 0),
                         in_=rW[ss][:, ii:ii + 1], in_offset=None,
                         bounds_check=regs["bcap"], oob_is_err=False)))(i, s, di),
                     waits=(), inc="sc")
        ms_wall = ("sc", cnt["sc"])
        ms_wsl = emit("sync", lambda s: s.dma_start(
            out=ws_sb, in_=ws_d.rearrange("(p k) o -> p (k o)", p=P)),
            waits=[ms_wall], inc="xw")

        ms_qpe = {}
        ms_eqa = {}
        ms_rqd = {}
        ms_mq = {}
        ms_qadd = {}
        for f in range(NDC):
            fb = f % 2
            if fb == 0:
                wins = [P2a[:, 0:512], P2a[:, 512:1024],
                        P2b[:, 0:512], P2b[:, 512:1024]]
            else:
                wins = [P2c[:, 0:512], P2c[:, 512:1024], P1a, P1b]

            def f_q(t, ff=f, ws=wins):
                last = None
                for d in range(NDC):
                    for wdw in range(4):
                        last = t.matmul(
                            ws[wdw],
                            W_sb[2][:, D * d + P * ff:D * d + P * ff + P],
                            xnT[:, T * d + 512 * wdw:T * d + 512 * wdw + 512],
                            start=d == 0, stop=d == NDC - 1, **MM)
                return last
            if f == 0:
                w = [ms_wq, ms_Mev[NDC - 1]]
            elif f == 1:
                w = [ms_idxevac, ms_widx]
            else:
                w = [ms_rqd[f - 2]]
            ms_qpe[f] = emit("pe", f_q, waits=w, inc="pe")

            def f_eq(s, ws=wins, fb_=fb, ff=f):
                last = None
                for wdw in range(4):
                    last = s.activation(escrB[fb_][:, 512 * wdw:512 * wdw + 512],
                                        ws[wdw], AF.Exp, bias=bqT[:, ff:ff + 1])
                return last
            w = [ms_qpe[f]]
            if f >= 2:
                w.append(ms_mq[f - 2])
            if f in (0, 1):
                w += [ms_kvpe[NC - 1], ms_kt[NC - 1], ms_mk[NC - 1]]
            ms_eqa[f] = emit("act", f_eq, waits=w, inc="act")

            def f_rq(s, ws=wins, fb_=fb, ff=f):
                last = None
                for wdw in range(4):
                    last = s.activation(
                        rscrB[fb_][:, 512 * wdw:512 * wdw + 512], ws[wdw],
                        AF.Relu, bias=bqT[:, ff:ff + 1])
                return last
            w = [ms_qpe[f]]
            if f >= 2:
                w.append(ms_qadd[f - 2])
            if f in (0, 1):
                w += [ms_kvpe[NC - 1], ms_kt[NC - 1], ms_mk[NC - 1]]
            ms_rqd[f] = emit("act", f_rq, waits=w, inc="act")

            w = [ms_eqa[f]]
            if f >= 2:
                w.append(ms_qadd[f - 2])
            if f in (0, 1):
                w += [ms_kvpe[NC - 1], ms_kt[NC - 1]]
            ms_mq[f] = emit(
                "dve",
                (lambda fb_: (lambda v: v.tensor_scalar_min(
                    mscrB[fb_], escrB[fb_], 1.0)))(fb),
                waits=w, inc="dve")

            ms_qadd[f] = emit(
                "dve",
                (lambda fb_, ff: (lambda v: v.tensor_tensor(
                    qT[:, T * ff:T * ff + T], rscrB[fb_], mscrB[fb_], ALU.add)))(fb, f),
                waits=[ms_mq[f], ms_rqd[f]], inc="dve")

        ms_Etr = {}
        ms_Etev = {}
        ms_Emm = {}
        ms_Eyev = {}
        ms_Eyst = {}
        ms_Exw = {}
        wg_usage[2] = ms_qpe[NDC - 1]
        emit_wg(2)
        tiles = []
        for j in range(NE):
            for kk in range(CAPT[j]):
                tiles.append((j, kk, KBASE[j] + kk))

        def emit_xw(tt):
            b = tt % 2
            kg = tiles[tt][2]
            w = []
            if tt == 0:
                w.append(ms_scall)
            if tt in (0, 1):
                w.append(ms_qpe[NDC - 1])
            if tt >= 2:
                w.append(ms_Etr[tt - 2])
            ms_Exw[tt] = emit(
                "sync",
                (lambda kg_, bb: (lambda s: s.dma_start(
                    out=xw_pp[bb], in_=xs_r[:, kg_:kg_ + 1, :])))(kg, b),
                waits=w, inc="xw")

        def emit_Etr(tt):
            b = tt % 2
            tr_bank = P2c_bf[:, 1024 * b:1024 * b + 1024]

            def f_tr(t, bb=b, trb=tr_bank):
                last = None
                for c in range(NDC):
                    last = t.matmul(trb[:, P * c:P * c + P],
                                    xw_pp[bb][:, P * c:P * c + P], ident,
                                    is_transpose=True, start=True,
                                    stop=True, **MM)
                return last
            w = [ms_Exw[tt]]
            if tt == 0:
                w += [ms_eqa[NDC - 1], ms_rqd[NDC - 1]]
            if tt >= 2:
                w.append(ms_Etev[tt - 2])
            ms_Etr[tt] = emit("pe", f_tr, waits=w, inc="pe")

        emit_xw(0)
        emit_xw(1)
        emit_Etr(0)
        emit_Etr(1)
        for tt, (j, kk, kg) in enumerate(tiles):
            b = tt % 2
            if tt + 2 < NTIL:
                emit_xw(tt + 2)
            tr_bank = P2c_bf[:, 1024 * b:1024 * b + 1024]
            w = [ms_Etr[tt]]
            if tt >= 2:
                w.append(ms_Emm[tt - 2])
            ms_Etev[tt] = emit(
                "dve",
                (lambda bb, trb=tr_bank: (lambda v: v.tensor_copy(
                    xg_pp[bb], trb)))(b),
                waits=w, inc="dve")

            ps = P2a if b == 0 else P2b

            def f_emm(t, bb=b, ps_=ps, jj=j):
                last = None
                for d in range(NDC):
                    t.matmul(ps_[:, 0:512],
                             xg_pp[bb][:, P * d:P * d + P],
                             W_sb[jj % 3][:, D * d:D * d + 512],
                             start=d == 0, stop=d == NDC - 1, **MM)
                    last = t.matmul(ps_[:, 512:1024],
                                    xg_pp[bb][:, P * d:P * d + P],
                                    W_sb[jj % 3][:, D * d + 512:D * d + 1024],
                                    start=d == 0, stop=d == NDC - 1, **MM)
                return last
            w = [ms_Etev[tt]]
            if kk == 0:
                w.append(ms_wg[j])
            if tt in (0, 1):
                w += [ms_eqa[NDC - 2], ms_rqd[NDC - 2]]
            if tt >= 2:
                w.append(ms_Eyev[tt - 2])
            ms_Emm[tt] = emit("pe", f_emm, waits=w, inc="pe")
            if tt + 2 < NTIL:
                emit_Etr(tt + 2)
            if j + 3 < NE and kk == CAPT[j] - 1:
                wg_usage[j + 3] = ms_Emm[tt]
                emit_wg(j + 3)

            w = [ms_Emm[tt]]
            if tt == 0:
                w.append(ms_wsl)
            if tt >= 2:
                w.append(ms_Eyst[tt - 2])
            ms_Eyev[tt] = emit(
                "dve",
                (lambda bb, ps_=ps, kg_=kg: (lambda v: v.tensor_scalar_mul(
                    yst[bb], ps_, ws_sb[:, kg_:kg_ + 1])))(b),
                waits=w, inc="dve")

            ms_Eyst[tt] = emit(
                "act",
                (lambda kg_, bb: (lambda s: s.dma_start(
                    out=yv_r[:, kg_:kg_ + 1, :], in_=yst[bb])))(kg, b),
                waits=[ms_Eyev[tt]], inc="ys")
        ms_ysall = ("ys", cnt["ys"])

        ms_dxl = {}
        ms_dev = {}
        ms_ga0 = {}
        ms_dga = {}
        ms_dst = {}
        for gi in range(NC):
            b = gi % 2
            il = gi % 4
            w = []
            if gi < 2:
                w = [ms_xn[NC - 2 + gi], ms_sq[NC - 2 + gi]]
            else:
                w = [ms_dev[gi - 2]]
            ms_dxl[gi] = emit(
                "sync",
                (lambda g_, bb: (lambda s: s.dma_start(
                    out=x_pp[bb], in_=x_d[P * g_:P * g_ + P, :])))(gi, b),
                waits=w, inc=f"x{b}")

            ps = P2a if b == 0 else P2b

            def f_j0(t, g_=gi, ps_=ps):
                t.matmul(ps_[:, 0:512], cb_sb[0:9, P * g_:P * g_ + P],
                         eb_sb[0:9, 0:512], start=True, stop=False, **MM)
                t.matmul(ps_[:, 512:1024], cb_sb[0:9, P * g_:P * g_ + P],
                         eb_sb[0:9, 512:1024], start=True, stop=False, **MM)
                last = None
                for d in range(NDC):
                    st = d == NDC - 1
                    t.matmul(ps_[:, 0:512],
                             qT[:, T * d + P * g_:T * d + P * g_ + P],
                             M_sb[:, D * d:D * d + 512],
                             start=False, stop=st, **MM)
                    last = t.matmul(ps_[:, 512:1024],
                                    qT[:, T * d + P * g_:T * d + P * g_ + P],
                                    M_sb[:, D * d + 512:D * d + 1024],
                                    start=False, stop=st, **MM)
                return last
            w = [ms_Eyev[NTIL - 2 + b]]
            if gi == 0:
                w += [ms_qadd[NDC - 1], ms_Mev[NDC - 1], ms_cb]
            if gi >= 2:
                w.append(ms_dev[gi - 2])
            mpe = emit("pe", f_j0, waits=w, inc="pe")

            def f_dev(v, g_=gi, il_=il, ps_=ps, b_=b):
                return v.tensor_tensor(acc_t[il_], ps_, x_pp[b_], ALU.add)
            w = [mpe, ms_dxl[gi]]
            if gi in (0, 1):
                w += [ms_xn[NC - 1], ms_sq[NC - 1]]
            if gi == 2:
                w.append(ms_mq[NDC - 2])
            if gi == 3:
                w.append(ms_mq[NDC - 1])
            if gi >= 4:
                w.append(ms_dst[gi - 4])
            ms_dev[gi] = emit("dve", f_dev, waits=w, inc="dve")

            w = [ms_dev[gi]]
            if gi == 0:
                w.append(ms_ysall)
            ms_ga0[gi] = emit(
                "pool",
                (lambda g_, il_: (lambda g: g.indirect_dma_start(
                    out=acc_t[il_], out_offset=None, in_=yv_d,
                    in_offset=IOA(d0i[:, g_:g_ + 1], 0),
                    bounds_check=regs["bcap"], oob_is_err=False,
                    compute_op=ALU.add)))(gi, il),
                waits=w, inc="ga")
            for gg in ([gi - 1] if gi >= 1 else []) + ([gi] if gi == NC - 1 else []):
                ms_dga[gg] = emit(
                    "pool",
                    (lambda g_, il_: (lambda g: g.indirect_dma_start(
                        out=acc_t[il_], out_offset=None, in_=yv_d,
                        in_offset=IOA(d1i[:, g_:g_ + 1], 0),
                        bounds_check=regs["bcap"], oob_is_err=False,
                        compute_op=ALU.add)))(gg, gg % 4),
                    waits=[ms_ga0[gg]], inc="ga")
                ms_dst[gg] = emit(
                    "sync",
                    (lambda g_, il_: (lambda s: s.dma_start(
                        out=y_d[P * g_:P * g_ + P, :], in_=acc_t[il_])))(gg, gg % 4),
                    waits=[ms_dga[gg]], inc="out")

        if debug:
            emit("sync", lambda s: s.dma_start(out=dbg["rank"], in_=ibRank),
                 waits=[ms_dch], inc="out")
            emit("sync", lambda s: s.dma_start(out=dbg["red"], in_=ib[:, 128:256]),
                 waits=[ms_dch], inc="out")
            emit("sync", lambda s: s.dma_start(out=dbg["d0i"], in_=d0i),
                 waits=[ms_dch], inc="out")
            emit("sync", lambda s: s.dma_start(out=dbg["d1i"], in_=d1i),
                 waits=[ms_dch], inc="out")
            emit("sync", lambda s: s.dma_start(out=dbg["widx"], in_=widx),
                 waits=[ms_widx], inc="out")
            emit("sync", lambda s: s.dma_start(out=dbg["srow"], in_=srow),
                 waits=[ms_dch], inc="out")
            emit("sync", lambda s: s.dma_start(out=dbg["msk"], in_=msk_all),
                 waits=[ms_dch], inc="out")
            emit("sync", lambda s: s.dma_start(out=dbg["s0"], in_=s0_all),
                 waits=[ms_dch], inc="out")
            emit("sync", lambda s: s.dma_start(out=dbg["r0"], in_=dbg_sb[:, 0:48]),
                 waits=[ms_dch], inc="out")
            emit("sync", lambda s: s.dma_start(out=dbg["ch"], in_=dbg_sb[:, 48:112]),
                 waits=[ms_dch], inc="out")
            emit("sync", lambda s: s.dma_start(out=dbg["vk"], in_=vk_sb),
                 waits=[ms_vk], inc="out")
            emit("sync", lambda s: s.dma_start(out=dbg["qT"], in_=qT),
                 waits=[ms_qadd[NDC - 1]], inc="out")
            emit("sync", lambda s: s.dma_start(out=dbg["M"], in_=M_sb),
                 waits=[ms_Mev[NDC - 1]], inc="out")

        final_out = cnt["out"]
        emit("sync", None, waits=[("out", final_out)])

        block = ent(nc.Block())

        def runner(eng_name):
            def run(eng):
                for waits, fn, minc in ops[eng_name]:
                    for (snm, val) in waits:
                        eng.wait_ge(sems[snm], val)
                    if fn is None:
                        continue
                    inst = fn(eng)
                    if minc is not None:
                        snm, val = minc
                        step = 16 if snm in DMA_SEMS else 1
                        inst.then_inc(sems[snm], step)
            return run

        block.tensor(runner("pe"))
        block.vector(runner("dve"))
        block.scalar(runner("act"))
        block.gpsimd(runner("pool"))
        block.sync(runner("sync"))

    return nc


_NC_CACHE = {}


def get_nc(debug=False):
    key = (debug,)
    if key not in _NC_CACHE:
        _NC_CACHE[key] = build_nc(debug)
    return _NC_CACHE[key]


def _prep_w(w):
    return np.ascontiguousarray(
        np.asarray(w, np.float32).reshape(8, 128, 1024)
        .transpose(1, 0, 2).reshape(128, 8192))


def make_in_maps(inputs):
    x = np.ascontiguousarray(np.asarray(inputs["L0_attn_x"], dtype=np.float32))
    cst = np.zeros((128, 280), np.float32)
    cst[:, 0:128] = np.triu(np.ones((128, 128), np.float32))
    cc = np.arange(16)
    cst[:, 128:256] = np.kron((cc[:, None] < cc[None, :]).astype(np.float32),
                              np.eye(8, dtype=np.float32))
    cst[0:8, 256:264] = np.eye(8, dtype=np.float32)
    cst[:, 264:272] = np.arange(8, dtype=np.float32)[None, :]
    cst[0:8, 272] = np.array(BASE, np.float32) - 1.0
    cst[0:8, 273] = np.array([c * 128 for c in CAPT], np.float32)
    cst[0:8, 274] = np.arange(8, dtype=np.float32)
    cst[:, 275] = np.arange(128, dtype=np.float32)
    cst[:, 276] = 1.0

    common = {
        "wq": _prep_w(inputs["L0_attn_Wq"]),
        "wk": _prep_w(inputs["L0_attn_Wk"]),
        "wv": _prep_w(inputs["L0_attn_Wv"]),
        "wp": _prep_w(inputs["L0_attn_Wp"]),
        "bq": np.ascontiguousarray(np.asarray(inputs["L0_attn_bq"], np.float32).reshape(1, D)),
        "gw": np.ascontiguousarray(np.asarray(inputs["L0_ffn_gate_w"], np.float32)),
        "smp": None,
        "gb": np.ascontiguousarray(np.asarray(inputs["L0_ffn_gate_b"], np.float32).reshape(1, 10)),
        "ew": np.ascontiguousarray(
            np.asarray(inputs["L0_ffn_expert_w"], np.float32)
            .reshape(8, 8, 128, 1024).transpose(0, 2, 1, 3).reshape(1024, 8192)),
        "cst": cst,
    }
    smp = np.zeros((65, D), np.float32)
    smp[0:8] = np.asarray(inputs["L0_ffn_expert_b"], np.float32)
    smp[8] = np.asarray(inputs["L0_attn_bp"], np.float32).reshape(D)
    smp[32] = np.asarray(inputs["L0_attn_bk"], np.float32).reshape(D)
    smp[64] = np.asarray(inputs["L0_attn_bv"], np.float32).reshape(D)
    common["smp"] = smp
    in_maps = []
    for c in range(8):
        m = dict(common)
        m["x"] = np.ascontiguousarray(x[c])
        in_maps.append(m)
    return in_maps


def kernel(**inputs):
    nc = get_nc(False)
    in_maps = make_in_maps(inputs)
    res = run_bass_kernel_spmd(nc, in_maps, core_ids=list(range(8)))
    out = np.stack([np.asarray(r["y"]) for r in res.results], axis=0)
    return out


# version 19
# speedup vs baseline: 2.2953x; 1.3432x over previous
import numpy as np
from contextlib import ExitStack

import concourse.bass as bass
import concourse.mybir as mybir
from concourse.bass import IndirectOffsetOnAxis as IOA
from concourse.bass_utils import run_bass_kernel_spmd

F32 = mybir.dt.float32
BF16 = mybir.dt.bfloat16
I32 = mybir.dt.int32
AF = mybir.ActivationFunctionType
ALU = mybir.AluOpType
AX = mybir.AxisListType

P = 128
T = 2048
D = 1024
NC = 16
NDC = 8
NH = 16
HD = 64
NE = 8
EPS = 1e-5
BIG = float(1 << 15)

CAPT = [14, 11, 9, 8, 7, 6, 5, 4]
KBASE = [0]
for c in CAPT[:-1]:
    KBASE.append(KBASE[-1] + c)
NTIL = sum(CAPT)
CAPSUM = NTIL * 128
BASE = [k * 128 for k in KBASE]

DMA_SEMS = ("x0", "x1", "w0", "w1", "w2", "init", "initg", "wpm", "sm",
            "out", "xg", "sc", "wg", "xw", "ys", "ga")


def build_nc(debug=False):
    nc = bass.Bass("TRN2", target_bir_lowering=False,
                   detect_race_conditions=False)

    x_d = nc.dram_tensor("x", [T, D], F32, kind="ExternalInput").ap()
    wq_d = nc.dram_tensor("wq", [P, NDC * D], F32, kind="ExternalInput").ap()
    wk_d = nc.dram_tensor("wk", [P, NDC * D], F32, kind="ExternalInput").ap()
    wv_d = nc.dram_tensor("wv", [P, NDC * D], F32, kind="ExternalInput").ap()
    wp_d = nc.dram_tensor("wp", [P, NDC * D], F32, kind="ExternalInput").ap()
    bq_d = nc.dram_tensor("bq", [1, D], F32, kind="ExternalInput").ap()
    gw_d = nc.dram_tensor("gw", [10, D], F32, kind="ExternalInput").ap()
    sm_d = nc.dram_tensor("smp", [65, D], F32, kind="ExternalInput").ap()
    gb_d = nc.dram_tensor("gb", [1, 10], F32, kind="ExternalInput").ap()
    ew_d = nc.dram_tensor("ew", [NE * P, NDC * D], F32, kind="ExternalInput").ap()
    cst_d = nc.dram_tensor("cst", [P, 280], F32, kind="ExternalInput").ap()
    y_d = nc.dram_tensor("y", [T, D], F32, kind="ExternalOutput").ap()
    kind = "ExternalOutput" if debug else "Internal"
    xng_d = nc.dram_tensor("xng", [T, D], BF16, kind=kind).ap()
    xs_d = nc.dram_tensor("xsort", [CAPSUM, D], BF16, kind=kind).ap()
    yv_d = nc.dram_tensor("yvals", [CAPSUM, D], F32, kind=kind).ap()
    ws_d = nc.dram_tensor("wsort", [CAPSUM, 1], F32, kind=kind).ap()
    xs_r = xs_d.rearrange("(p k) g -> p k g", p=P)
    yv_r = yv_d.rearrange("(p k) g -> p k g", p=P)
    dbg = {}
    if debug:
        dbg["rank"] = nc.dram_tensor("dbg_rank", [P, 128], F32, kind="ExternalOutput").ap()
        dbg["red"] = nc.dram_tensor("dbg_red", [P, 128], F32, kind="ExternalOutput").ap()
        dbg["d0i"] = nc.dram_tensor("dbg_d0i", [P, NC], I32, kind="ExternalOutput").ap()
        dbg["d1i"] = nc.dram_tensor("dbg_d1i", [P, NC], I32, kind="ExternalOutput").ap()
        dbg["widx"] = nc.dram_tensor("dbg_widx", [P, 8], I32, kind="ExternalOutput").ap()
        dbg["srow"] = nc.dram_tensor("dbg_srow", [8, 384], F32, kind="ExternalOutput").ap()
        dbg["msk"] = nc.dram_tensor("dbg_msk", [P, 128], F32, kind="ExternalOutput").ap()
        dbg["s0"] = nc.dram_tensor("dbg_s0", [P, 128], F32, kind="ExternalOutput").ap()
        dbg["r0"] = nc.dram_tensor("dbg_r0", [P, 48], F32, kind="ExternalOutput").ap()
        dbg["ch"] = nc.dram_tensor("dbg_ch", [P, 64], F32, kind="ExternalOutput").ap()
        dbg["vk"] = nc.dram_tensor("dbg_vk", [P, 512], BF16, kind="ExternalOutput").ap()
        dbg["qT"] = nc.dram_tensor("dbg_qT", [P, NDC * T], BF16, kind="ExternalOutput").ap()
        dbg["M"] = nc.dram_tensor("dbg_M", [P, NDC * D], BF16, kind="ExternalOutput").ap()

    with ExitStack() as ctx:
        ent = ctx.enter_context

        xnT_h = ent(nc.sbuf_tensor("xnT", [P, NDC * T], BF16))
        xnT = xnT_h[:]
        xnT_f32 = xnT_h[:].bitcast(F32)
        qT = ent(nc.sbuf_tensor("qT", [P, NDC * T], BF16))[:]
        M_sb = ent(nc.sbuf_tensor("M_sb", [P, NDC * D], BF16))[:]
        W_sb = [ent(nc.sbuf_tensor(f"W{b}", [P, NDC * D], BF16))[:] for b in range(3)]
        acc_h = ent(nc.sbuf_tensor("acc", [P, 8 * D], F32))
        acc = acc_h[:]
        acc_bf = acc_h[:].bitcast(BF16)
        x_pp = [ent(nc.sbuf_tensor(f"xpp{b}", [P, D], F32))[:] for b in range(2)]
        xn_bf = [ent(nc.sbuf_tensor(f"xnbf{b}", [P, D], BF16))[:] for b in range(2)]
        ws_sb = ent(nc.sbuf_tensor("ws_sb", [P, NTIL], F32))[:]
        xr2 = [ent(nc.sbuf_tensor(f"xr2{b}", [P, D], BF16))[:] for b in range(2)]
        xw_pp = [xnT[:, 0:D], xnT[:, D:2 * D]]
        xg_pp = [xnT[:, 2 * D:3 * D], xnT[:, 3 * D:4 * D]]
        yst = [xnT_f32[:, 2048:3072], xnT_f32[:, 3072:4096]]
        gw_bf = ent(nc.sbuf_tensor("gw_bf", [10, D], BF16))[:]
        bq8_bf = ent(nc.sbuf_tensor("bq8_bf", [NDC, P], BF16))[:]
        vk_sb = ent(nc.sbuf_tensor("vk_sb", [P, 512], BF16))[:]
        cb_sb = ent(nc.sbuf_tensor("cb_sb", [9, T], BF16))[:]
        c_all = ent(nc.sbuf_tensor("c_all", [P, NC * NE], F32))[:]
        c_bf = ent(nc.sbuf_tensor("c_bf", [P, NC * NE], BF16))[:]
        msk_all = ent(nc.sbuf_tensor("msk_all", [P, NC * NE], F32))[:]
        s0_all = ent(nc.sbuf_tensor("s0_all", [P, NC * NE], F32))[:]
        ib = ent(nc.sbuf_tensor("ib", [P, 880], F32))[:]
        dbg_sb = ent(nc.sbuf_tensor("dbg_sb", [P, 112], F32))[:] if debug else None
        rw_sb = ent(nc.sbuf_tensor("rw_sb", [P, 32], F32))[:]
        d0i = ent(nc.sbuf_tensor("d0i", [P, NC], I32))[:]
        d1i = ent(nc.sbuf_tensor("d1i", [P, NC], I32))[:]
        widx = ent(nc.sbuf_tensor("widx", [P, 8], I32))[:]
        srow = ent(nc.sbuf_tensor("srow", [8, 384], F32))[:]
        tcol = ent(nc.sbuf_tensor("tcol", [P, 1], F32))[:]
        cst_sb = ent(nc.sbuf_tensor("cst_sb", [P, 280], F32))[:]
        gwT = qT[:, 0:80]
        ones = qT[0:65, 80:208]
        zero = qT[0:1, 208:720]
        bqT = ent(nc.sbuf_tensor("bqT", [P, NDC], F32))[:]
        sm_sb = ent(nc.sbuf_tensor("sm_sb", [65, D], BF16))[:]
        gb_bf = ent(nc.sbuf_tensor("gb_bf", [1, 10], BF16))[:]
        eb_sb = sm_sb[0:9, :]
        bk_bf = sm_sb[32:33, :]
        bv_bf = sm_sb[64:65, :]
        ident = ent(nc.sbuf_tensor("ident", [P, P], BF16))[:]
        cst_eps = ent(nc.sbuf_tensor("cst_eps", [P, 1], F32))[:]
        nc.const_aps.register(F32, EPS, cst_eps)
        egt = [ib[:, 0:10], ib[:, 10:20]]
        pt = [ib[:, 20:30], ib[:, 30:40]]
        m8t = [ib[:, 40:48], ib[:, 48:56]]
        wtst = [ib[:, 56:64], ib[:, 64:72]]
        sm = {}
        for ni, nm in enumerate(("sum", "mean", "ssq", "ssqn", "nvar", "lnv",
                                 "rstd", "m1", "nm1", "Z", "rz", "den", "ds",
                                 "rd")):
            sm[nm] = [ib[:, 72 + 2 * ni + b:73 + 2 * ni + b] for b in range(2)]

        cL = cst_sb[:, 0:128]
        cPM = cst_sb[:, 128:256]
        cIdc = cst_sb[0:8, 256:264]
        cErow = cst_sb[0:8, 264:272]
        cBasem1 = cst_sb[0:8, 272:273]
        cCap = cst_sb[0:8, 273:274]
        cEcol = cst_sb[0:8, 274:275]
        cPcol = cst_sb[:, 275:276]
        cOnecol = cst_sb[:, 276:277]

        sTot = srow[0:1, 0:128]
        sCarry = srow[0:1, 128:256]
        sLoads = srow[0:1, 256:264]
        sKeyrow = srow[0:1, 264:272]
        sSigrow = srow[0:1, 272:280]
        sRankrow = srow[0:1, 280:288]
        sKeyT = srow[0:8, 288:289]
        sRankT = srow[0:8, 289:290]
        sM2 = srow[0:8, 292:300]
        sG2T = srow[0:8, 300:308]
        sSigcol = srow[0:8, 308:309]
        sBrow = srow[0:1, 312:320]
        sCrow = srow[0:1, 320:328]
        sTmp8 = srow[0:8, 328:336]

        ibRank = ib[:, 0:128]
        ibBrep = ib[:, 128:256]
        ibCrep = ib[:, 256:384]
        ibT1 = ib[:, 384:512]
        ibBad = ib[:, 512:640]
        ibPr = ib[:, 640:768]
        rR = [ib[:, 768:784]] * 2
        rAny = [ib[:, 784:800]] * 2
        rBad = [ib[:, 800:816]] * 2
        mR = ib[:, 816:832]
        mAny = ib[:, 832:848]
        mBad = ib[:, 848:864]
        mW = ib[:, 864:880]
        iPm = ib[:, 816:832]
        iKm = ib[:, 832:848]
        iK2 = ib[:, 848:864]
        iRp = ib[:, 832:848]
        iU = ib[:, 864:880]
        iT4 = ib[:, 816:832]
        iRpf = ib[:, 864:880]
        widxf = ib[:, 816:824]
        rW = [rw_sb[:, 0:16], rw_sb[:, 16:32]]

        sq_thr = acc[:, 0:D]
        xm_scr = acc[:, D:2 * D]
        def abf(c0, n):
            return acc_bf[:, c0:c0 + n]
        escr = [abf(4096 + 1024 * b, D) for b in range(2)]
        rscr = [abf(6144 + 1024 * b, D) for b in range(2)]
        mscr = [abf(8192 + 1024 * b, D) for b in range(2)]
        kscr = [abf(10240 + 1024 * b, D) for b in range(2)]
        vscr = [abf(12288 + 1024 * b, D) for b in range(2)]
        escrB = [abf(4096 + 2048 * b, 2048) for b in range(2)]
        rscrB = [abf(8192 + 2048 * b, 2048) for b in range(2)]
        mscrB = [abf(12288 + 2048 * b, 2048) for b in range(2)]
        acc_t = [acc[:, D * il:D * il + D] for il in range(4)]

        P2a = ent(nc.psum_tensor("P2a", [P, D], F32))[:]
        P2b = ent(nc.psum_tensor("P2b", [P, D], F32))[:]
        P2c_h = ent(nc.psum_tensor("P2c", [P, D], F32))
        P2c = P2c_h[:]
        P2c_bf = P2c_h[:].bitcast(BF16)
        P1a = ent(nc.psum_tensor("P1a", [P, 512], F32))[:]
        P1b_h = ent(nc.psum_tensor("P1b", [P, 512], F32))
        P1b = P1b_h[:]
        P1b_bf = P1b_h[:].bitcast(BF16)

        sems = {}
        for nm in ("pe", "dve", "act", "pool") + DMA_SEMS:
            sems[nm] = ent(nc.semaphore(f"s_{nm}"))
        sems["dvex"] = ent(nc.semaphore("s_dvex"))
        sems["actx"] = ent(nc.semaphore("s_actx"))
        _cutc = {"dvex": 0, "actx": 0}

        def mkcut(semname):
            def cut(eng, inst):
                _cutc[semname] += 1
                inst.then_inc(sems[semname], 1)
                eng.wait_ge(sems[semname], _cutc[semname])
            return cut
        vcut = mkcut("dvex")
        acut = mkcut("actx")

        ops = {"pe": [], "dve": [], "act": [], "pool": [], "sync": []}
        cnt = {nm: 0 for nm in ("pe", "dve", "act", "pool") + DMA_SEMS}

        def emit(eng, fn, waits=(), inc=None):
            m = None
            if inc is not None:
                cnt[inc] += 16 if inc in DMA_SEMS else 1
                m = (inc, cnt[inc])
            ops[eng].append((tuple(waits), fn, m))
            return m

        MM = dict(skip_group_check=True)

        regs = {}

        def f_regs(g):
            regs["bcap"] = g.to_reg(CAPSUM - 1)
            regs["bew"] = g.to_reg(NE * P - 1)
            return None
        emit("pool", f_regs)

        def f_init_pool0(g):
            g.memset(ones, 1.0)
            g.memset(zero, 0.0)
            g.memset(cst_eps, EPS)
            g.memset(cb_sb, 1.0)
            return g.memset(ident, 0.0)
        ms_initp0 = emit("pool", f_init_pool0, inc="pool")

        def f_init_pool(g):
            return g.affine_select(
                out=ident, in_=ident, compare_op=ALU.not_equal, fill=1.0,
                base=0, pattern=[[-1, P]], channel_multiplier=1)
        ms_initp = emit("pool", f_init_pool, waits=[ms_initp0], inc="pool")

        ms_cstl = emit("sync", lambda s: s.dma_start(out=cst_sb, in_=cst_d),
                       inc="init")

        ms_gw = emit("pool", lambda g: g.dma_start(out=gw_bf, in_=gw_d),
                     inc="initg")
        emit("pool", lambda g: g.dma_start(
            out=bq8_bf, in_=bq_d.rearrange("o (c p) -> (o c) p", p=P)),
            inc="initg")
        ms_initg2 = ("initg", cnt["initg"])
        ms_wk = emit("pool", lambda g: g.dma_start(out=W_sb[0], in_=wk_d),
                     inc="w0")
        ms_wv = emit("pool", lambda g: g.dma_start(out=W_sb[1], in_=wv_d),
                     inc="w1")
        ms_sm = emit("pool", lambda g: g.dma_start(out=sm_sb, in_=sm_d),
                     inc="sm")
        emit("pool", lambda g: g.dma_start(out=gb_bf, in_=gb_d), inc="init")
        init_all = ("init", cnt["init"])
        ms_wq = emit("pool", lambda g: g.dma_start(out=W_sb[2], in_=wq_d),
                     inc="w2")
        ms_wpm = emit("pool", lambda g: g.dma_start(out=M_sb, in_=wp_d),
                      inc="wpm")

        def f_gwT(t):
            last = None
            for c in range(NDC):
                last = t.matmul(P1b_bf[:, 10 * c:10 * c + 10],
                                gw_bf[0:10, P * c:P * c + P], ident[0:10, 0:10],
                                is_transpose=True, start=True, stop=True, **MM)
            return last
        def f_bqT(t):
            return t.matmul(P1b_bf[:, 128:128 + NDC], bq8_bf,
                            ident[0:NDC, 0:NDC],
                            is_transpose=True, start=True, stop=True, **MM)

        def emit_gwT():
            pe = emit("pe", f_gwT, waits=[ms_gw, ms_initp], inc="pe")
            pe2 = emit("pe", f_bqT, waits=[ms_initg2], inc="pe")
            emit("dve", lambda v: v.tensor_copy(bqT, P1b_bf[:, 128:128 + NDC]),
                 waits=[pe2], inc="dve")
            return emit("dve", lambda v: v.tensor_copy(gwT, P1b_bf[:, 0:80]),
                        waits=[pe], inc="dve")

        ms_xload = {}
        ms_xn = {}
        ms_sq = {}
        ms_trPE = {}
        ms_trEvac = {}
        ms_kpe = {}
        ms_vpe = {}
        ms_gpe = {}
        ms_ek = {}
        ms_rk = {}
        ms_mk = {}
        ms_kt = {}
        ms_vcp = {}
        ms_kvpe = {}
        ms_gd1 = {}
        ms_eg = {}
        ms_gd2 = {}
        ms_xg = {}

        def xnT_sl(d, ii):
            return xnT[:, T * d + P * ii: T * d + P * ii + P]

        def e_front(i):
            b = i % 2
            w = []
            if i >= 2:
                w = [ms_xn[i - 2], ms_sq[i - 2]]
            ms_xload[i] = emit(
                "sync",
                (lambda ii, bb: (lambda s: s.dma_start(
                    out=x_pp[bb], in_=x_d[P * ii:P * ii + P, :])))(i, b),
                waits=w, inc=f"x{b}")

            def f_ln1(v, bb=b):
                vcut(v, v.reduce_sum(sm["sum"][bb], x_pp[bb], AX.X))
                return v.tensor_scalar_mul(sm["mean"][bb], sm["sum"][bb], 1.0 / D)
            emit("dve", f_ln1, waits=[ms_xload[i]])

            ms_sq[i] = emit(
                "act",
                (lambda bb: (lambda s: s.activation(
                    sq_thr, x_pp[bb], AF.Square, accum_out=sm["ssq"][bb])))(b),
                waits=[ms_xload[i]], inc="act")

            def f_ln2(v, bb=b):
                vcut(v, v.tensor_scalar_mul(sm["ssqn"][bb], sm["ssq"][bb], 1.0 / D))
                return v.scalar_tensor_tensor(
                    sm["nvar"][bb], sm["mean"][bb], sm["mean"][bb], sm["ssqn"][bb],
                    ALU.mult, ALU.subtract)
            ms_nvar = emit("dve", f_ln2, waits=[ms_sq[i]], inc="dve")

            def f_rstd(s, bb=b):
                acut(s, s.activation(sm["lnv"][bb], sm["nvar"][bb], AF.Ln,
                                     scale=-1.0, bias=EPS))
                return s.activation(sm["rstd"][bb], sm["lnv"][bb], AF.Exp,
                                    scale=-0.5)
            w = [ms_nvar]
            if i == 0:
                w.append(ms_initp)
            ms_rstd = emit("act", f_rstd, waits=w, inc="act")

            def f_norm(v, bb=b):
                v.tensor_scalar_sub(xm_scr, x_pp[bb], sm["mean"][bb])
                return v.tensor_scalar_mul(xn_bf[bb], xm_scr, sm["rstd"][bb])
            w = [ms_rstd]
            if i >= 2:
                w.append(ms_trPE[i - 2])
                w.append(ms_xg[i - 2])
            ms_xn[i] = emit("dve", f_norm, waits=w, inc="dve")

            ms_xg[i] = emit(
                "act",
                (lambda ii, bb: (lambda s: s.dma_start(
                    out=xng_d[P * ii:P * ii + P, :], in_=xn_bf[bb])))(i, b),
                waits=[ms_xn[i]], inc="xg")

        def e_tr(i):
            b = i % 2
            tr_bank = P2c_bf[:, 1024 * b:1024 * b + 1024]

            def f_tr(t, bb=b, trb=tr_bank):
                last = None
                for c in range(NDC):
                    last = t.matmul(trb[:, P * c:P * c + P],
                                    xn_bf[bb][:, P * c:P * c + P], ident,
                                    is_transpose=True, start=True, stop=True, **MM)
                return last
            w = [ms_xn[i]]
            if i >= 2:
                w.append(ms_trEvac[i - 2])
            ms_trPE[i] = emit("pe", f_tr, waits=w, inc="pe")

            xnT_dst = xnT.rearrange("p (d t) -> p d t", t=T)[:, :, P * i:P * i + P]

            def f_trev(v, trb=tr_bank, dst=xnT_dst):
                src_ = trb.rearrange("p (d t) -> p d t", t=P)
                return v.tensor_copy(dst, src_)
            ms_trEvac[i] = emit("dve", f_trev, waits=[ms_trPE[i]], inc="dve")

        def e_kv(i):
            b = i % 2

            def f_kv(t, ii=i, bb=b):
                last = None
                if ii == 0:
                    t.matmul(P1a, ones[0:1, 0:P], zero[0:1, 0:512],
                             start=True, stop=False, **MM)
                for h in range(NH):
                    pr = h // 2
                    par = h % 2
                    out = P1a[64 * par:64 * par + 64, 64 * pr:64 * pr + 64]
                    last = t.matmul(out, vscr[bb][:, HD * h:HD * h + HD],
                                    kscr[bb][:, HD * h:HD * h + HD],
                                    start=False,
                                    stop=(ii == NC - 1 and h == NH - 1), **MM)
                return last
            ms_kvpe[i] = emit("pe", f_kv, waits=[ms_kt[i], ms_vcp[i]], inc="pe")

        def e_mms(i):
            b = i % 2

            def f_g(t, ii=i):
                t.matmul(P1b[:, 0:10], ones[0:1, 0:P], gb_bf[0:1, 0:10],
                         start=True, stop=False, **MM)
                last = None
                for d in range(NDC):
                    last = t.matmul(P1b[:, 0:10], xnT_sl(d, ii),
                                    gwT[:, 10 * d:10 * d + 10],
                                    start=False, stop=d == NDC - 1, **MM)
                return last

            def f_k(t, ii=i):
                last = None
                for d in range(NDC):
                    t.matmul(P2a[:, 0:512], xnT_sl(d, ii),
                             W_sb[0][:, D * d:D * d + 512],
                             start=d == 0, stop=False, **MM)
                    last = t.matmul(P2a[:, 512:1024], xnT_sl(d, ii),
                                    W_sb[0][:, D * d + 512:D * d + 1024],
                                    start=d == 0, stop=False, **MM)
                return last

            def f_kb(t):
                t.matmul(P2a[:, 0:512], ones[32:33, 0:P], bk_bf[0:1, 0:512],
                         start=False, stop=True, **MM)
                return t.matmul(P2a[:, 512:1024], ones[32:33, 0:P],
                                bk_bf[0:1, 512:1024],
                                start=False, stop=True, **MM)

            def f_v(t, ii=i):
                last = None
                for d in range(NDC):
                    t.matmul(P2b[:, 0:512], xnT_sl(d, ii),
                             W_sb[1][:, D * d:D * d + 512],
                             start=d == 0, stop=False, **MM)
                    last = t.matmul(P2b[:, 512:1024], xnT_sl(d, ii),
                                    W_sb[1][:, D * d + 512:D * d + 1024],
                                    start=d == 0, stop=False, **MM)
                return last

            def f_vb(t):
                t.matmul(P2b[:, 0:512], ones[64:65, 0:P], bv_bf[0:1, 0:512],
                         start=False, stop=True, **MM)
                return t.matmul(P2b[:, 512:1024], ones[64:65, 0:P],
                                bv_bf[0:1, 512:1024],
                                start=False, stop=True, **MM)

            w = [ms_trEvac[i]]
            if i == 0:
                w += [ms_wk]
            else:
                w += [ms_rk[i - 1]]
            emit("pe", f_k, waits=w, inc="pe")
            ms_kpe[i] = emit("pe", f_kb,
                             waits=[ms_sm] if i == 0 else (), inc="pe")
            w = [ms_trEvac[i]]
            if i == 0:
                w.append(ms_wv)
            else:
                w.append(ms_vcp[i - 1])
            emit("pe", f_v, waits=w, inc="pe")
            ms_vpe[i] = emit("pe", f_vb,
                             waits=[ms_sm] if i == 0 else (), inc="pe")
            w = [ms_trEvac[i]]
            if i == 0:
                w += [ms_gwT, init_all]
            else:
                w += [ms_eg[i - 1], ms_gd1[i - 1]]
            ms_gpe[i] = emit("pe", f_g, waits=w, inc="pe")

        def e_evacs(i):
            b = i % 2
            ms_ek[i] = emit(
                "act",
                (lambda bb: (lambda s: s.activation(escr[bb], P2a, AF.Exp)))(b),
                waits=[ms_kpe[i]] + ([ms_mk[i - 2]] if i >= 2 else []),
                inc="act")
            ms_rk[i] = emit(
                "act",
                (lambda bb: (lambda s: s.activation(rscr[bb], P2a, AF.Relu)))(b),
                waits=[ms_kpe[i]], inc="act")
            ms_mk[i] = emit(
                "dve",
                (lambda bb: (lambda v: v.tensor_scalar_min(mscr[bb], escr[bb], 1.0)))(b),
                waits=[ms_ek[i]] + ([ms_kt[i - 2]] if i >= 2 else []),
                inc="dve")
            w = [ms_mk[i], ms_rk[i]]
            if i >= 2:
                w.append(ms_kvpe[i - 2])
            ms_kt[i] = emit(
                "dve",
                (lambda bb: (lambda v: v.tensor_tensor(
                    kscr[bb], rscr[bb], mscr[bb], ALU.add)))(b),
                waits=w, inc="dve")
            w = [ms_vpe[i]]
            if i >= 2:
                w.append(ms_kvpe[i - 2])
            ms_vcp[i] = emit(
                "act",
                (lambda bb: (lambda s: s.copy(vscr[bb], P2b)))(b),
                waits=w, inc="act")

        def e_gate(i):
            b = i % 2

            def f_gd1(v, bb=b):
                vcut(v, v.reduce_max(sm["m1"][bb], P1b[:, 0:10], AX.X))
                return v.tensor_scalar_mul(sm["nm1"][bb], sm["m1"][bb], -1.0)
            ms_gd1[i] = emit("dve", f_gd1, waits=[ms_gpe[i]], inc="dve")

            w = [ms_gd1[i]]
            if i >= 2:
                w.append(ms_gd2[i - 2])
            ms_eg[i] = emit(
                "act",
                (lambda bb: (lambda s: s.activation(
                    egt[bb], P1b[:, 0:10], AF.Exp, bias=sm["nm1"][bb])))(b),
                waits=w, inc="act")

            def f_gd2(v, bb=b, ii=i):
                vcut(v, v.reduce_sum(sm["Z"][bb], egt[bb], AX.X))
                vcut(v, v.reciprocal(sm["rz"][bb], sm["Z"][bb]))
                v.tensor_scalar_mul(pt[bb], egt[bb], sm["rz"][bb])
                vcut(v, v.max(m8t[bb], pt[bb]))
                v.tensor_single_scalar(msk_all[:, 8 * ii:8 * ii + 8],
                                       pt[bb][:, 0:8],
                                       m8t[bb][:, 1:2], ALU.is_ge)
                v.tensor_single_scalar(s0_all[:, 8 * ii:8 * ii + 8],
                                       pt[bb][:, 0:8],
                                       m8t[bb][:, 0:1], ALU.is_ge)
                v.tensor_tensor(wtst[bb], msk_all[:, 8 * ii:8 * ii + 8],
                                pt[bb][:, 0:8], ALU.mult)
                vcut(v, v.reduce_sum(sm["den"][bb], wtst[bb], AX.X))
                vcut(v, v.tensor_scalar_max(sm["ds"][bb], sm["den"][bb], 1e-30))
                vcut(v, v.reciprocal(sm["rd"][bb], sm["ds"][bb]))
                v.tensor_scalar_mul(c_all[:, 8 * ii:8 * ii + 8], wtst[bb],
                                    sm["rd"][bb])
                return v.tensor_copy(c_bf[:, 8 * ii:8 * ii + 8],
                                     c_all[:, 8 * ii:8 * ii + 8])
            ms_gd2[i] = emit("dve", f_gd2, waits=[ms_eg[i]], inc="dve")

        e_front(0)
        e_front(1)
        e_tr(0)
        ms_gwT = emit_gwT()
        for i in range(NC):
            if i + 2 < NC:
                e_front(i + 2)
            if i >= 1:
                e_kv(i - 1)
            if i + 1 < NC:
                e_tr(i + 1)
            e_mms(i)
            e_evacs(i)
            e_gate(i)
        e_kv(NC - 1)

        def f_ct(t):
            last = None
            for i in range(NC):
                last = t.matmul(P2c_bf[0:8, P * i:P * i + P],
                                c_bf[:, 8 * i:8 * i + 8], ident,
                                is_transpose=True, start=True, stop=True, **MM)
            return last
        ms_ctpe = emit("pe", f_ct,
                       waits=[ms_gd2[NC - 1], ms_trEvac[NC - 1],
                              ms_trEvac[NC - 2]],
                       inc="pe")
        ms_cb = emit("dve", lambda v: v.tensor_copy(cb_sb[0:8, :], P2c_bf[0:8, :]),
                     waits=[ms_ctpe, ms_initp0], inc="dve")
        ms_vk = emit("dve", lambda v: v.tensor_copy(vk_sb, P1a),
                     waits=[ms_kvpe[NC - 1]], inc="dve")

        ms_Mpe = {}
        ms_Mev = {}

        def emit_C(pr):
            ps = P2a if pr % 2 == 0 else P2b

            def f_M(t, pp=pr, ps_=ps):
                last = None
                for par in range(2):
                    r0 = 64 * par
                    for hh in range(2):
                        last = t.matmul(
                            ps_[r0:r0 + 64, 512 * hh:512 * hh + 512],
                            vk_sb[r0:r0 + 64, 64 * pp:64 * pp + 64],
                            M_sb[r0:r0 + 64, D * pp + 512 * hh:D * pp + 512 * hh + 512],
                            start=True, stop=True, **MM)
                return last
            if pr == 0:
                w = [ms_vk, ms_wpm, ms_rk[NC - 1]]
            elif pr == 1:
                w = [ms_vcp[NC - 1]]
            else:
                w = [ms_Mev[pr - 2]]
            ms_Mpe[pr] = emit("pe", f_M, waits=w, inc="pe")
            ms_Mev[pr] = emit(
                "dve",
                (lambda pp, ps_: (lambda v: v.tensor_copy(
                    M_sb[:, D * pp:D * pp + D], ps_)))(pr, ps),
                waits=[ms_Mpe[pr]], inc="dve")

        emit_C(0)
        ms_i1 = emit("pe", lambda t: t.matmul(
            P2c[0:1, 0:128], cOnecol, msk_all, start=True, stop=True, **MM),
            waits=[ms_cb, ms_gd2[NC - 1], ms_cstl], inc="pe")
        ms_i2 = emit("dve", lambda v: v.tensor_copy(sTot, P2c[0:1, 0:128]),
                     waits=[ms_i1], inc="dve")
        emit_C(1)
        ms_i3 = emit("pe", lambda t: t.matmul(
            P2c[:, 128:129], sTot, cIdc[0:1, 0:1],
            is_transpose=True, start=True, stop=True, **MM),
            waits=[ms_i2], inc="pe")
        ms_i4 = emit("dve", lambda v: v.tensor_copy(tcol, P2c[:, 128:129]),
                     waits=[ms_i3], inc="dve")
        emit_C(2)
        ms_i5 = emit("pe", lambda t: t.matmul(
            P2c[0:1, 256:384], tcol, cPM, start=True, stop=True, **MM),
            waits=[ms_i4], inc="pe")

        def f_i6(v):
            vcut(v, v.tensor_copy(sCarry, P2c[0:1, 256:384]))
            vcut(v, v.tensor_tensor(sLoads, sCarry[0:1, 120:128],
                                    sTot[0:1, 120:128], ALU.add))
            vcut(v, v.tensor_scalar_mul(sKeyrow, sLoads, 8.0))
            return v.tensor_tensor(sKeyrow, sKeyrow, cErow[0:1, 0:8], ALU.add)
        ms_i6 = emit("dve", f_i6, waits=[ms_i5], inc="dve")
        emit_C(3)

        def f_rank(t):
            last = None
            for c in range(NC):
                t.matmul(P1a[:, 8 * c:8 * c + 8], cL[0:1, 0:128],
                         sCarry[0:1, 8 * c:8 * c + 8],
                         start=True, stop=False, **MM)
                last = t.matmul(P1a[:, 8 * c:8 * c + 8], cL,
                                msk_all[:, 8 * c:8 * c + 8],
                                start=False, stop=True, **MM)
            return last
        ms_i7 = emit("pe", f_rank, waits=[ms_i6, ms_vk], inc="pe")

        ms_i8 = emit("pe", lambda t: t.matmul(
            P2c[0:8, 132:133], sKeyrow, cIdc[0:1, 0:1],
            is_transpose=True, start=True, stop=True, **MM),
            waits=[ms_i6], inc="pe")
        ms_i9 = emit("pe", lambda t: t.matmul(
            P2c[0:8, 136:144], cL[0:1, 0:8], sKeyrow,
            start=True, stop=True, **MM),
            waits=[ms_i6], inc="pe")

        def f_i10(v):
            vcut(v, v.tensor_copy(sKeyT, P2c[0:8, 132:133]))
            vcut(v, v.tensor_single_scalar(sM2, P2c[0:8, 136:144], sKeyT,
                                           ALU.is_gt))
            return v.reduce_sum(sRankT, sM2, AX.X)
        ms_i10 = emit("dve", f_i10, waits=[ms_i8, ms_i9], inc="dve")
        emit_C(4)

        ms_i11 = emit("pe", lambda t: t.matmul(
            P2c[0:1, 160:168], sRankT, cIdc, is_transpose=True,
            start=True, stop=True, **MM), waits=[ms_i10], inc="pe")
        ms_i12 = emit("dve", lambda v: v.tensor_copy(sRankrow, P2c[0:1, 160:168]),
                      waits=[ms_i11], inc="dve")
        emit_C(5)
        ms_i13 = emit("pe", lambda t: t.matmul(
            P2c[0:8, 168:176], cL[0:1, 0:8], sRankrow,
            start=True, stop=True, **MM), waits=[ms_i12], inc="pe")

        def f_i14(v):
            vcut(v, v.tensor_single_scalar(sG2T, P2c[0:8, 168:176], cEcol,
                                           ALU.is_equal))
            vcut(v, v.tensor_tensor(sTmp8, sG2T, cErow, ALU.mult))
            return v.reduce_sum(sSigcol, sTmp8, AX.X)
        ms_i14 = emit("dve", f_i14, waits=[ms_i13], inc="dve")
        emit_C(6)

        ms_i15 = emit("pe", lambda t: t.matmul(
            P2c[0:1, 176:184], sSigcol, cIdc, is_transpose=True,
            start=True, stop=True, **MM), waits=[ms_i14], inc="pe")
        ms_i16 = emit("dve", lambda v: v.tensor_copy(sSigrow, P2c[0:1, 176:184]),
                      waits=[ms_i15], inc="dve")
        emit_C(7)
        ms_i17 = emit("pe", lambda t: t.matmul(
            P1b[:, 0:8], cL[0:1, 0:128], sSigrow, start=True, stop=True, **MM),
            waits=[ms_i16, ms_gd2[NC - 1], ms_eg[NC - 1]], inc="pe")

        def f_i18(v):
            vcut(v, v.tensor_scalar_mul(widxf, P1b[:, 0:8], 128.0))
            vcut(v, v.tensor_single_scalar(ib[:, 824:832], widxf, cPcol,
                                           ALU.add))
            return v.tensor_copy(widx, ib[:, 824:832])
        ms_widx = emit("dve", f_i18, waits=[ms_i17], inc="dve")

        ms_i19 = emit("pe", lambda t: t.matmul(
            P2c[0:1, 184:192], cBasem1, sG2T, start=True, stop=True, **MM),
            waits=[ms_i14], inc="pe")
        ms_i20 = emit("pe", lambda t: t.matmul(
            P2c[0:1, 192:200], cCap, sG2T, start=True, stop=True, **MM),
            waits=[ms_i14], inc="pe")

        def f_i21(v):
            vcut(v, v.tensor_copy(sBrow, P2c[0:1, 184:192]))
            return v.tensor_copy(sCrow, P2c[0:1, 192:200])
        ms_i21 = emit("dve", f_i21, waits=[ms_i19, ms_i20], inc="dve")

        def f_i22(t):
            last = None
            for c in range(NC):
                t.matmul(P1b[:, 128 + 8 * c:128 + 8 * c + 8],
                         cL[0:1, 0:128], sBrow, start=True, stop=True, **MM)
                last = t.matmul(P1b[:, 256 + 8 * c:256 + 8 * c + 8],
                                cL[0:1, 0:128], sCrow,
                                start=True, stop=True, **MM)
            return last
        ms_i22 = emit("pe", f_i22, waits=[ms_i21], inc="pe")
        ms_i23 = ms_i22

        def f_i24(v):
            v.tensor_copy(ibRank, P1a[:, 0:128])
            v.tensor_copy(ibBrep, P1b[:, 128:256])
            return v.tensor_copy(ibCrep, P1b[:, 256:384])
        ms_idxevac = emit("dve", f_i24, waits=[ms_i7, ms_i22, ms_i23],
                          inc="dve")

        def seg(ap):
            return ap.rearrange("p (k e) -> p k e", e=8)

        def f_redset(v, msks, oR, oAny, oBad, oW):
            v.tensor_tensor(ibPr, msks, ibT1, ALU.mult)
            vcut(v, v.reduce_sum(oR, seg(ibPr), AX.X))
            v.reduce_sum(oAny, seg(msks), AX.X)
            v.tensor_tensor(ibPr, msks, ibBad, ALU.mult)
            vcut(v, v.reduce_sum(oBad, seg(ibPr), AX.X))
            v.tensor_tensor(ibPr, msks, c_all, ALU.mult)
            vcut(v, v.reduce_sum(oW, seg(ibPr), AX.X))

        def f_rpchain(v, s):
            v.tensor_scalar_mul(iPm, rR[s], 1.0 / 128.0)
            v.tensor_scalar_add(iKm, iPm, -0.49609375)
            v.tensor_scalar_add(iK2, iKm, 12582912.0)
            v.tensor_scalar_add(iKm, iK2, -12582912.0)
            v.scalar_tensor_tensor(iPm, iKm, -128.0, rR[s],
                                   ALU.mult, ALU.add)
            v.scalar_tensor_tensor(iK2, iPm, 64.0, iKm,
                                   ALU.mult, ALU.add)
            v.tensor_tensor(iU, rBad[s], rAny[s], ALU.subtract)
            v.scalar_tensor_tensor(iT4, iU, BIG, iK2,
                                   ALU.mult, ALU.add)
            v.tensor_scalar_add(iRpf, iT4, BIG)
            v.tensor_scalar_mul(iKm, iRpf, -1.0)
            v.tensor_tensor(iT4, iKm, iRpf, ALU.max)
            last = v.tensor_copy(d0i if s == 0 else d1i, iT4)
            return last

        def f_dch(v):
            v.tensor_tensor(ibT1, ibRank, ibBrep, ALU.add)
            v.tensor_tensor(ibBad, ibRank, ibCrep, ALU.is_gt)
            f_redset(v, s0_all, rR[0], rAny[0], rBad[0], rW[0])
            if debug:
                vcut(v, v.tensor_copy(dbg_sb[:, 0:48], ib[:, 768:816]))
            f_rpchain(v, 0)
            if debug:
                vcut(v, v.tensor_copy(dbg_sb[:, 48:112], ib[:, 816:880]))
            f_redset(v, msk_all, mR, mAny, mBad, mW)
            v.tensor_tensor(rW[1], mW, rW[0], ALU.subtract)
            v.tensor_tensor(rR[1], mR, rR[0], ALU.subtract)
            v.tensor_tensor(rAny[1], mAny, rAny[0], ALU.subtract)
            vcut(v, v.tensor_tensor(rBad[1], mBad, rBad[0], ALU.subtract))
            return f_rpchain(v, 1)
        ms_dch = emit("dve", f_dch, waits=[ms_idxevac], inc="dve")

        ms_wg = {}
        wg_usage = {0: ms_kpe[NC - 1], 1: ms_vpe[NC - 1]}

        def emit_wg(j):
            w = [ms_widx, wg_usage[j]]
            ms_wg[j] = emit(
                "pool",
                (lambda jj: (lambda g: g.indirect_dma_start(
                    out=W_sb[jj % 3], out_offset=None, in_=ew_d,
                    in_offset=IOA(widx[:, jj:jj + 1], 0),
                    bounds_check=regs["bew"], oob_is_err=False)))(j),
                waits=w, inc="wg")

        emit_wg(0)
        emit_wg(1)

        rl_buf = [xn_bf[0], xn_bf[1], xr2[0], xr2[1]]
        ms_rl = {}
        ms_sct = {}
        for i in range(NC):
            b = i % 4
            if i < 2:
                w = [ms_xg[NC - 2 + i], ms_trPE[NC - 2 + i]]
            elif i < 4:
                w = [ms_xg[i]]
            else:
                w = [ms_xg[i], ms_sct[i - 4]]
            ms_rl[i] = emit(
                "sync",
                (lambda ii, bb: (lambda s: s.dma_start(
                    out=rl_buf[bb], in_=xng_d[P * ii:P * ii + P, :])))(i, b),
                waits=w, inc=f"x{i % 2}")
            for s in range(2):
                di = d0i if s == 0 else d1i
                w2 = [ms_rl[i]] + ([ms_dch] if (i == 0 and s == 0) else [])
                ms_sct[i] = emit(
                    "pool",
                    (lambda ii, bb, di_: (lambda g: g.indirect_dma_start(
                        out=xs_d, out_offset=IOA(di_[:, ii:ii + 1], 0),
                        in_=rl_buf[bb], in_offset=None,
                        bounds_check=regs["bcap"], oob_is_err=False)))(i, b, di),
                    waits=w2, inc="sc")
        ms_scall = ("sc", cnt["sc"])
        for i in range(NC):
            for s in range(2):
                di = d0i if s == 0 else d1i
                emit("pool",
                     (lambda ii, ss, di_: (lambda g: g.indirect_dma_start(
                         out=ws_d, out_offset=IOA(di_[:, ii:ii + 1], 0),
                         in_=rW[ss][:, ii:ii + 1], in_offset=None,
                         bounds_check=regs["bcap"], oob_is_err=False)))(i, s, di),
                     waits=(), inc="sc")
        ms_wall = ("sc", cnt["sc"])
        ms_wsl = emit("act", lambda s: s.dma_start(
            out=ws_sb, in_=ws_d.rearrange("(p k) o -> p (k o)", p=P)),
            waits=[ms_wall], inc="xw")

        ms_qpe = {}
        ms_eqa = {}
        ms_rqd = {}
        ms_mq = {}
        ms_qadd = {}
        for f in range(NDC):
            fb = f % 2
            if fb == 0:
                wins = [P2a[:, 0:512], P2a[:, 512:1024],
                        P2b[:, 0:512], P2b[:, 512:1024]]
            else:
                wins = [P2c[:, 0:512], P2c[:, 512:1024], P1a, P1b]

            def f_q(t, ff=f, ws=wins):
                last = None
                for d in range(NDC):
                    for wdw in range(4):
                        last = t.matmul(
                            ws[wdw],
                            W_sb[2][:, D * d + P * ff:D * d + P * ff + P],
                            xnT[:, T * d + 512 * wdw:T * d + 512 * wdw + 512],
                            start=d == 0, stop=d == NDC - 1, **MM)
                return last
            if f == 0:
                w = [ms_wq, ms_Mev[NDC - 1]]
            elif f == 1:
                w = [ms_idxevac, ms_widx]
            else:
                w = [ms_rqd[f - 2]]
            ms_qpe[f] = emit("pe", f_q, waits=w, inc="pe")

            def f_eq(s, ws=wins, fb_=fb, ff=f):
                last = None
                for wdw in range(4):
                    last = s.activation(escrB[fb_][:, 512 * wdw:512 * wdw + 512],
                                        ws[wdw], AF.Exp, bias=bqT[:, ff:ff + 1])
                return last
            w = [ms_qpe[f]]
            if f >= 2:
                w.append(ms_mq[f - 2])
            if f in (0, 1):
                w += [ms_kvpe[NC - 1], ms_kt[NC - 1], ms_mk[NC - 1]]
            ms_eqa[f] = emit("act", f_eq, waits=w, inc="act")

            def f_rq(s, ws=wins, fb_=fb, ff=f):
                last = None
                for wdw in range(4):
                    last = s.activation(
                        rscrB[fb_][:, 512 * wdw:512 * wdw + 512], ws[wdw],
                        AF.Relu, bias=bqT[:, ff:ff + 1])
                return last
            w = [ms_qpe[f]]
            if f >= 2:
                w.append(ms_qadd[f - 2])
            if f in (0, 1):
                w += [ms_kvpe[NC - 1], ms_kt[NC - 1], ms_mk[NC - 1]]
            ms_rqd[f] = emit("act", f_rq, waits=w, inc="act")

            w = [ms_eqa[f]]
            if f >= 2:
                w.append(ms_qadd[f - 2])
            if f in (0, 1):
                w += [ms_kvpe[NC - 1], ms_kt[NC - 1]]
            ms_mq[f] = emit(
                "dve",
                (lambda fb_: (lambda v: v.tensor_scalar_min(
                    mscrB[fb_], escrB[fb_], 1.0)))(fb),
                waits=w, inc="dve")

            ms_qadd[f] = emit(
                "dve",
                (lambda fb_, ff: (lambda v: v.tensor_tensor(
                    qT[:, T * ff:T * ff + T], rscrB[fb_], mscrB[fb_], ALU.add)))(fb, f),
                waits=[ms_mq[f], ms_rqd[f]], inc="dve")

        ms_Etr = {}
        ms_Etev = {}
        ms_Emm = {}
        ms_Eyev = {}
        ms_Eyst = {}
        ms_Exw = {}
        wg_usage[2] = ms_qpe[NDC - 1]
        emit_wg(2)
        tiles = []
        for j in range(NE):
            for kk in range(CAPT[j]):
                tiles.append((j, kk, KBASE[j] + kk))

        def emit_xw(tt):
            b = tt % 2
            kg = tiles[tt][2]
            w = []
            if tt == 0:
                w.append(ms_scall)
            if tt in (0, 1):
                w.append(ms_qpe[NDC - 1])
            if tt >= 2:
                w.append(ms_Etr[tt - 2])
            ms_Exw[tt] = emit(
                "sync",
                (lambda kg_, bb: (lambda s: s.dma_start(
                    out=xw_pp[bb], in_=xs_r[:, kg_:kg_ + 1, :])))(kg, b),
                waits=w, inc="xw")

        def emit_Etr(tt):
            b = tt % 2
            tr_bank = P2c_bf[:, 1024 * b:1024 * b + 1024]

            def f_tr(t, bb=b, trb=tr_bank):
                last = None
                for c in range(NDC):
                    last = t.matmul(trb[:, P * c:P * c + P],
                                    xw_pp[bb][:, P * c:P * c + P], ident,
                                    is_transpose=True, start=True,
                                    stop=True, **MM)
                return last
            w = [ms_Exw[tt]]
            if tt == 0:
                w += [ms_eqa[NDC - 1], ms_rqd[NDC - 1]]
            if tt >= 2:
                w.append(ms_Etev[tt - 2])
            ms_Etr[tt] = emit("pe", f_tr, waits=w, inc="pe")

        emit_xw(0)
        emit_xw(1)
        emit_Etr(0)
        emit_Etr(1)
        for tt, (j, kk, kg) in enumerate(tiles):
            b = tt % 2
            if tt + 2 < NTIL:
                emit_xw(tt + 2)
            tr_bank = P2c_bf[:, 1024 * b:1024 * b + 1024]
            w = [ms_Etr[tt]]
            if tt >= 2:
                w.append(ms_Emm[tt - 2])
            ms_Etev[tt] = emit(
                "dve",
                (lambda bb, trb=tr_bank: (lambda v: v.tensor_copy(
                    xg_pp[bb], trb)))(b),
                waits=w, inc="dve")

            ps = P2a if b == 0 else P2b

            def f_emm(t, bb=b, ps_=ps, jj=j):
                last = None
                for d in range(NDC):
                    t.matmul(ps_[:, 0:512],
                             xg_pp[bb][:, P * d:P * d + P],
                             W_sb[jj % 3][:, D * d:D * d + 512],
                             start=d == 0, stop=d == NDC - 1, **MM)
                    last = t.matmul(ps_[:, 512:1024],
                                    xg_pp[bb][:, P * d:P * d + P],
                                    W_sb[jj % 3][:, D * d + 512:D * d + 1024],
                                    start=d == 0, stop=d == NDC - 1, **MM)
                return last
            w = [ms_Etev[tt]]
            if kk == 0:
                w.append(ms_wg[j])
            if tt in (0, 1):
                w += [ms_eqa[NDC - 2], ms_rqd[NDC - 2]]
            if tt >= 2:
                w.append(ms_Eyev[tt - 2])
            ms_Emm[tt] = emit("pe", f_emm, waits=w, inc="pe")
            if tt + 2 < NTIL:
                emit_Etr(tt + 2)
            if j + 3 < NE and kk == CAPT[j] - 1:
                wg_usage[j + 3] = ms_Emm[tt]
                emit_wg(j + 3)

            for ee in ([tt - 1] if tt >= 1 else []) + ([tt] if tt == NTIL - 1 else []):
                eb_ = ee % 2
                eps_ = P2a if eb_ == 0 else P2b
                ekg = tiles[ee][2]
                w = [ms_Emm[ee]]
                if ee == 0:
                    w.append(ms_wsl)
                if ee >= 2:
                    w.append(ms_Eyst[ee - 2])
                ms_Eyev[ee] = emit(
                    "dve",
                    (lambda bb, ps2, kg_: (lambda v: v.tensor_scalar_mul(
                        yst[bb], ps2, ws_sb[:, kg_:kg_ + 1])))(eb_, eps_, ekg),
                    waits=w, inc="dve")
                ms_Eyst[ee] = emit(
                    "act",
                    (lambda kg_, bb: (lambda s: s.dma_start(
                        out=yv_r[:, kg_:kg_ + 1, :], in_=yst[bb])))(ekg, eb_),
                    waits=[ms_Eyev[ee]], inc="ys")
        ms_ysall = ("ys", cnt["ys"])

        ms_dxl = {}
        ms_dev = {}
        ms_ga0 = {}
        ms_dga = {}
        ms_dst = {}
        xdp = [xnT_f32[:, 4096 + 1024 * k:5120 + 1024 * k] for k in range(4)]
        for gi in range(NC):
            b = gi % 2
            il = gi % 4
            if gi < 4:
                w = [ms_qpe[NDC - 1]]
            else:
                w = [ms_dev[gi - 4]]
            ms_dxl[gi] = emit(
                "sync",
                (lambda g_, kk_: (lambda s: s.dma_start(
                    out=xdp[kk_], in_=x_d[P * g_:P * g_ + P, :])))(gi, il),
                waits=w, inc=f"x{b}")

            ps = P2a if b == 0 else P2b

            def f_j0(t, g_=gi, ps_=ps):
                t.matmul(ps_[:, 0:512], cb_sb[0:9, P * g_:P * g_ + P],
                         eb_sb[0:9, 0:512], start=True, stop=False, **MM)
                t.matmul(ps_[:, 512:1024], cb_sb[0:9, P * g_:P * g_ + P],
                         eb_sb[0:9, 512:1024], start=True, stop=False, **MM)
                last = None
                for d in range(NDC):
                    st = d == NDC - 1
                    t.matmul(ps_[:, 0:512],
                             qT[:, T * d + P * g_:T * d + P * g_ + P],
                             M_sb[:, D * d:D * d + 512],
                             start=False, stop=st, **MM)
                    last = t.matmul(ps_[:, 512:1024],
                                    qT[:, T * d + P * g_:T * d + P * g_ + P],
                                    M_sb[:, D * d + 512:D * d + 1024],
                                    start=False, stop=st, **MM)
                return last
            w = [ms_Eyev[NTIL - 2 + b]]
            if gi == 0:
                w += [ms_qadd[NDC - 1], ms_Mev[NDC - 1], ms_cb]
            if gi >= 2:
                w.append(ms_dev[gi - 2])
            mpe = emit("pe", f_j0, waits=w, inc="pe")

            def f_dev(v, g_=gi, il_=il, ps_=ps):
                return v.tensor_tensor(acc_t[il_], ps_, xdp[il_], ALU.add)
            w = [mpe, ms_dxl[gi]]
            if gi in (0, 1):
                w += [ms_xn[NC - 1], ms_sq[NC - 1]]
            if gi == 2:
                w.append(ms_mq[NDC - 2])
            if gi == 3:
                w.append(ms_mq[NDC - 1])
            if gi >= 4:
                w.append(ms_dst[gi - 4])
            ms_dev[gi] = emit("dve", f_dev, waits=w, inc="dve")

            w = [ms_dev[gi]]
            if gi == 0:
                w.append(ms_ysall)
            ms_ga0[gi] = emit(
                "pool",
                (lambda g_, il_: (lambda g: g.indirect_dma_start(
                    out=acc_t[il_], out_offset=None, in_=yv_d,
                    in_offset=IOA(d0i[:, g_:g_ + 1], 0),
                    bounds_check=regs["bcap"], oob_is_err=False,
                    compute_op=ALU.add)))(gi, il),
                waits=w, inc="ga")
            for gg in ([gi - 2] if gi >= 2 else []) + ([gi - 1, gi] if gi == NC - 1 else []):
                ms_dga[gg] = emit(
                    "pool",
                    (lambda g_, il_: (lambda g: g.indirect_dma_start(
                        out=acc_t[il_], out_offset=None, in_=yv_d,
                        in_offset=IOA(d1i[:, g_:g_ + 1], 0),
                        bounds_check=regs["bcap"], oob_is_err=False,
                        compute_op=ALU.add)))(gg, gg % 4),
                    waits=[ms_ga0[gg]], inc="ga")
                ms_dst[gg] = emit(
                    "act",
                    (lambda g_, il_: (lambda s: s.dma_start(
                        out=y_d[P * g_:P * g_ + P, :], in_=acc_t[il_])))(gg, gg % 4),
                    waits=[ms_dga[gg]], inc="out")

        if debug:
            emit("sync", lambda s: s.dma_start(out=dbg["rank"], in_=ibRank),
                 waits=[ms_dch], inc="out")
            emit("sync", lambda s: s.dma_start(out=dbg["red"], in_=ib[:, 128:256]),
                 waits=[ms_dch], inc="out")
            emit("sync", lambda s: s.dma_start(out=dbg["d0i"], in_=d0i),
                 waits=[ms_dch], inc="out")
            emit("sync", lambda s: s.dma_start(out=dbg["d1i"], in_=d1i),
                 waits=[ms_dch], inc="out")
            emit("sync", lambda s: s.dma_start(out=dbg["widx"], in_=widx),
                 waits=[ms_widx], inc="out")
            emit("sync", lambda s: s.dma_start(out=dbg["srow"], in_=srow),
                 waits=[ms_dch], inc="out")
            emit("sync", lambda s: s.dma_start(out=dbg["msk"], in_=msk_all),
                 waits=[ms_dch], inc="out")
            emit("sync", lambda s: s.dma_start(out=dbg["s0"], in_=s0_all),
                 waits=[ms_dch], inc="out")
            emit("sync", lambda s: s.dma_start(out=dbg["r0"], in_=dbg_sb[:, 0:48]),
                 waits=[ms_dch], inc="out")
            emit("sync", lambda s: s.dma_start(out=dbg["ch"], in_=dbg_sb[:, 48:112]),
                 waits=[ms_dch], inc="out")
            emit("sync", lambda s: s.dma_start(out=dbg["vk"], in_=vk_sb),
                 waits=[ms_vk], inc="out")
            emit("sync", lambda s: s.dma_start(out=dbg["qT"], in_=qT),
                 waits=[ms_qadd[NDC - 1]], inc="out")
            emit("sync", lambda s: s.dma_start(out=dbg["M"], in_=M_sb),
                 waits=[ms_Mev[NDC - 1]], inc="out")

        final_out = cnt["out"]
        emit("sync", None, waits=[("out", final_out)])

        block = ent(nc.Block())

        def runner(eng_name):
            def run(eng):
                for waits, fn, minc in ops[eng_name]:
                    for (snm, val) in waits:
                        eng.wait_ge(sems[snm], val)
                    if fn is None:
                        continue
                    inst = fn(eng)
                    if minc is not None:
                        snm, val = minc
                        step = 16 if snm in DMA_SEMS else 1
                        inst.then_inc(sems[snm], step)
            return run

        block.tensor(runner("pe"))
        block.vector(runner("dve"))
        block.scalar(runner("act"))
        block.gpsimd(runner("pool"))
        block.sync(runner("sync"))

    return nc


_NC_CACHE = {}


def get_nc(debug=False):
    key = (debug,)
    if key not in _NC_CACHE:
        _NC_CACHE[key] = build_nc(debug)
    return _NC_CACHE[key]


def _prep_w(w):
    return np.ascontiguousarray(
        np.asarray(w, np.float32).reshape(8, 128, 1024)
        .transpose(1, 0, 2).reshape(128, 8192))


def make_in_maps(inputs):
    x = np.ascontiguousarray(np.asarray(inputs["L0_attn_x"], dtype=np.float32))
    cst = np.zeros((128, 280), np.float32)
    cst[:, 0:128] = np.triu(np.ones((128, 128), np.float32))
    cc = np.arange(16)
    cst[:, 128:256] = np.kron((cc[:, None] < cc[None, :]).astype(np.float32),
                              np.eye(8, dtype=np.float32))
    cst[0:8, 256:264] = np.eye(8, dtype=np.float32)
    cst[:, 264:272] = np.arange(8, dtype=np.float32)[None, :]
    cst[0:8, 272] = np.array(BASE, np.float32) - 1.0
    cst[0:8, 273] = np.array([c * 128 for c in CAPT], np.float32)
    cst[0:8, 274] = np.arange(8, dtype=np.float32)
    cst[:, 275] = np.arange(128, dtype=np.float32)
    cst[:, 276] = 1.0

    common = {
        "wq": _prep_w(inputs["L0_attn_Wq"]),
        "wk": _prep_w(inputs["L0_attn_Wk"]),
        "wv": _prep_w(inputs["L0_attn_Wv"]),
        "wp": _prep_w(inputs["L0_attn_Wp"]),
        "bq": np.ascontiguousarray(np.asarray(inputs["L0_attn_bq"], np.float32).reshape(1, D)),
        "gw": np.ascontiguousarray(np.asarray(inputs["L0_ffn_gate_w"], np.float32)),
        "smp": None,
        "gb": np.ascontiguousarray(np.asarray(inputs["L0_ffn_gate_b"], np.float32).reshape(1, 10)),
        "ew": np.ascontiguousarray(
            np.asarray(inputs["L0_ffn_expert_w"], np.float32)
            .reshape(8, 8, 128, 1024).transpose(0, 2, 1, 3).reshape(1024, 8192)),
        "cst": cst,
    }
    smp = np.zeros((65, D), np.float32)
    smp[0:8] = np.asarray(inputs["L0_ffn_expert_b"], np.float32)
    smp[8] = np.asarray(inputs["L0_attn_bp"], np.float32).reshape(D)
    smp[32] = np.asarray(inputs["L0_attn_bk"], np.float32).reshape(D)
    smp[64] = np.asarray(inputs["L0_attn_bv"], np.float32).reshape(D)
    common["smp"] = smp
    in_maps = []
    for c in range(8):
        m = dict(common)
        m["x"] = np.ascontiguousarray(x[c])
        in_maps.append(m)
    return in_maps


def kernel(**inputs):
    nc = get_nc(False)
    in_maps = make_in_maps(inputs)
    res = run_bass_kernel_spmd(nc, in_maps, core_ids=list(range(8)))
    out = np.stack([np.asarray(r["y"]) for r in res.results], axis=0)
    return out
